# version 02
# baseline (speedup 1.0000x reference)
import math
from contextlib import ExitStack

import numpy as np
import ml_dtypes

import concourse.bass as bass
import concourse.mybir as mybir
from concourse.bass_utils import run_bass_kernel_spmd

F32 = mybir.dt.float32
BF16 = mybir.dt.bfloat16
U8 = mybir.dt.uint8
AF = mybir.ActivationFunctionType
ALU = mybir.AluOpType
AX = mybir.AxisListType

B, S, E = 4, 2048, 4096
H, KVH, D = 32, 8, 128
NB = 8
NUM_REMAIN = 410
INV_SQRT_D = 1.0 / math.sqrt(D)
NEG_SENT = -512.0
EPS = 1e-5

DEBUG = False


def wj(j):
    return 256 * (j + 1)


class SemStore:

    def __init__(self, nc, ctx):
        self.sems = {}
        self.counts = {}
        for n in set(Sched.ENG_SEMS.values()) | set(Sched.DMA_SEMS.values()):
            self.sems[n] = ctx.enter_context(nc.semaphore(n))
            self.counts[n] = 0
        self.tokidx = 0
        self.needed = set()


class Sched:
    ENG_SEMS = {"t": "et", "v": "ev", "s": "es", "g": "eg", "y": "ey"}
    DMA_SEMS = {"y": "dy", "g": "dg", "s": "ds"}

    def __init__(self, nc, store, tag):
        self.nc = nc
        self.tag = tag
        self.store = store
        self.items = []
        self.counts = store.counts
        self.sems = store.sems
        self.needed = set()

    def _add(self, eng, fn, after, sem_name, inc):
        waits = []
        for tok in after or []:
            if tok is None:
                continue
            self.store.needed.add(tok[3])
            waits.append(tok)
        self.counts[sem_name] += inc
        tokidx = self.store.tokidx
        self.store.tokidx += 1
        tok = (sem_name, self.counts[sem_name], eng, tokidx)
        self.items.append((eng, fn, waits, sem_name, inc, tokidx, tok))
        return tok

    def add(self, eng, fn, after=None):
        return self._add(eng, fn, after, self.ENG_SEMS[eng], 1)

    def dma(self, eng, fn, after=None):
        return self._add(eng, fn, after, self.DMA_SEMS[eng], 16)

    def emit(self, block_name):
        nc = self.nc
        with nc.Block(block_name) as blk:
            bodies = {"t": blk.tensor, "v": blk.vector, "s": blk.scalar,
                      "g": blk.gpsimd, "y": blk.sync}
            for ek, deco in bodies.items():
                my = [it for it in self.items if it[0] == ek]
                if not my:
                    continue

                def make(my_items, ek=ek):
                    def body(eng):
                        last_seen = {}
                        own = Sched.ENG_SEMS[ek]
                        for (_, fn, waits, sem_name, inc, tokidx, tok) in my_items:
                            ws = {}
                            for (sn, val, peng, ptok) in waits:
                                if ws.get(sn, -1) < val:
                                    ws[sn] = val
                            for sn, val in ws.items():
                                if last_seen.get(sn, -1) >= val:
                                    continue
                                eng.wait_ge(self.sems[sn], val)
                                last_seen[sn] = val
                            inst = fn(eng)
                            if inc == 1 and ek in ("t", "v", "s"):
                                inst = eng.drain()
                            if inst is None:
                                inst = eng.nop()
                            inst.then_inc(self.sems[sem_name], inc)
                    return body

                deco(make(my))
        return self


def build_program(n_phases=99, debug=DEBUG, h_lim=H):
    nc = bass.Bass("TRN2", target_bir_lowering=False)
    dt = nc.dram_tensor

    def inp(name, shape, dtype=F32):
        return dt(name, shape, dtype, kind="ExternalInput").ap()

    xt_own = inp("xt_own", [128, 32 * 1024])
    xt_kv = inp("xt_kv", [2, 128, 32 * 1024])
    x_own = inp("x_own", [NB, 128, 4096])
    wq_t = inp("wq_t", [32, 128, 4096])
    wk_t = inp("wk_t", [8, 128, 4096])
    wv_t = inp("wv_t", [8, 128, 4096])
    wo_t = inp("wo_t", [8, 128, 16384])
    wup_t = inp("wup_t", [8, 128, 16384])
    p1_in = inp("p1_t", [128, 4096])
    p2_in = inp("p2_t", [128, 4096])
    b1_in = inp("b1", [128, H])
    b2_in = inp("b2", [128, H])
    cosT_own_in = inp("cosT_own", [128, 1024])
    sinT2_own_in = inp("sinT2_own", [128, 1024])
    cosT_kv_in = inp("cosT_kv", [128, 2048])
    sinT2_kv_in = inp("sinT2_kv", [128, 2048])
    ident_in = inp("ident_in", [128, 128])
    minv_in = inp("minv", [NB, 128, 256], U8)
    m01j0_in = inp("m01j0", [128, 256])
    exp832_in = inp("exp832", [8, 32])
    gwT_in = inp("gw_t", [128, 128])
    ebias_in = inp("ebias", [128, 4])
    loraAT_in = inp("lat_t", [128, 1024])
    loraBp_in = inp("loraBp", [32, 4096])

    out = dt("out", [NB, 128, 4096], F32, kind="ExternalOutput").ap()
    qr_sp = dt("qr_sp", [H, 128, 1024], BF16, kind="Internal").ap()
    qh_sp = dt("qh_sp", [H, 128, 1024], BF16, kind="Internal").ap()
    o_sp = dt("o_sp", [H, 128, 1024], BF16, kind="Internal").ap()
    x2_sp = dt("x2_sp", [NB, 128, 4096], F32, kind="Internal").ap()
    x3_sp = dt("x3_sp", [NB, 128, 4096], F32, kind="Internal").ap()
    nrm_sp = dt("nrm_sp", [32, 1024], F32, kind="Internal").ap()
    k2_sp = dt("k2_sp", [8, 2048], F32, kind="Internal").ap()

    dbg = {}

    def dbgout(name, shape, dtype):
        if debug:
            dbg[name] = dt("dbg_" + name, shape, dtype, kind="ExternalOutput").ap()
            return dbg[name]
        return None

    d_qr = dbgout("qr", [H, 128, 1024], BF16)
    d_qh = dbgout("qh", [H, 128, 1024], BF16)
    d_nrm = dbgout("nrm", [32, 1024], F32)
    d_krT = dbgout("krT", [128, 8 * 2048], BF16)
    d_v = dbgout("v", [128, 16 * 1024], BF16)
    d_k2 = dbgout("k2", [32, 1], F32)
    d_biasT = dbgout("biasT", [128, NB * 32], F32)
    d_k2r = dbgout("k2r", [8, 2048], F32)
    d_k2m = dbgout("k2m", [2, 8, 1], F32)
    d_kh = dbgout("kh", [H, 128, 2048], BF16)
    d_o = dbgout("o", [H, 128, 1024], BF16)
    d_x2 = dbgout("x2", [NB, 128, 4096], F32)
    d_x3 = dbgout("x3", [NB, 128, 4096], F32)
    d_zm = dbgout("zm", [32, NB * 128], BF16)
    d_zms = dbgout("zms", [128, NB * 32], BF16)
    d_oh = dbgout("oh", [NB, 128, 8], F32)
    d_lg = dbgout("lg", [NB, 128, 8], F32)
    d_ps = dbgout("ps", [128, 64], F32)
    d_lat = dbgout("lat", [2, 128, 1024], BF16)
    d_tau = dbgout("tau", [NB, 128, H], F32)
    d_rr = dbgout("rr", [NB, 128, H], F32)
    d_praw = dbgout("praw", [128, 2048], BF16)
    d_pp = dbgout("pp", [128, 2048], BF16)
    d_z = dbgout("z", [128, 8], F32)
    d_m01 = dbgout("m01", [128, 2048], BF16)
    d_scb = dbgout("scb", [128, 2048], F32)

    octx = ExitStack()
    with octx:
        def mk(ctx_):
            def sbf(name, shape, dtype):
                return ctx_.enter_context(nc.sbuf_tensor(name, shape, dtype))
            return sbf
        sb = mk(octx)
        psum = lambda name, shape, dtype=F32: octx.enter_context(nc.psum_tensor(name, shape, dtype))
        store = SemStore(nc, octx)

        ident_b = sb("ident_b", [128, 128], BF16)
        p1b = sb("p1b", [128, H * 128], BF16)
        p2b = sb("p2b", [128, H * 128], BF16)
        b1_s = sb("b1_s", [128, H], F32)
        b2_s = sb("b2_s", [128, H], F32)
        m01j0 = sb("m01j0_s", [128, 256], BF16)
        minv_s = sb("minv_s", [128, NB * 256], U8)
        neg512 = sb("neg512", [128, 256], BF16)
        zeros_b = sb("zeros_b", [128, 2048], BF16)
        ones_b = sb("ones_b", [128, 1], BF16)
        exp832_b = sb("exp832_b", [8, 32], BF16)
        biasT = sb("biasT", [128, NB * 32], F32)

        ps_a = psum("ps_a", [128, 512], F32)
        ps_b = psum("ps_b", [128, 512], F32)
        ps_c = psum("ps_c", [128, 512], F32)
        ps_d = psum("ps_d", [128, 512], F32)
        ps_t = psum("ps_t", [128, 1024], BF16)
        ps_n = psum("ps_n", [1, 512], F32)
        ps_s = psum("ps_s", [128, 128], F32)
        ps4 = [ps_a, ps_b, ps_c, ps_d]

        sc = Sched(nc, store, "p0")
        consts_done = []
        with ExitStack() as p0ctx:
            stg0 = mk(p0ctx)("stg0", [128, 4096], F32)
            for fn in [
                lambda e: e.dma_start(out=b1_s[:], in_=b1_in),
                lambda e: e.dma_start(out=b2_s[:], in_=b2_in),
                lambda e: e.dma_start(out=minv_s[:].rearrange("p (n c) -> p n c", n=NB),
                                      in_=minv_in.rearrange("n p c -> p n c")),
            ]:
                consts_done.append(sc.dma("y", fn))

            def load_cast(dram_ap, dest_ap, after=None, stg_ap=None):
                stg = stg_ap if stg_ap is not None else stg0[:]
                tl = sc.dma("y", lambda e: e.dma_start(out=stg, in_=dram_ap),
                            after=after or [])
                return sc.add("v", lambda e: e.tensor_copy(out=dest_ap, in_=stg),
                              after=[tl])

            t1 = load_cast(ident_in, ident_b[:], stg_ap=stg0[:, 0:128])
            t2 = load_cast(p1_in, p1b[:], after=[t1])
            t3 = load_cast(p2_in, p2b[:], after=[t2])
            t4 = load_cast(m01j0_in, m01j0[:], after=[t3], stg_ap=stg0[:, 0:256])
            t5 = load_cast(exp832_in, exp832_b[:], after=[t4], stg_ap=stg0[0:8, 0:32])
            consts_done += [t1, t2, t3, t4, t5]

            def memsets(e):
                e.memset(neg512[:], NEG_SENT)
                e.memset(zeros_b[:], 0.0)
                return e.memset(ones_b[:], 1.0)
            consts_done.append(sc.add("v", memsets))
            sc.emit("ph0")

        if n_phases >= 1:
            with ExitStack() as p1ctx:
                sbl = mk(p1ctx)
                xb = sbl("xb", [128, 32 * 1024], BF16)
                wstg = sbl("wstg1", [128, 4096], F32)
                wload = sbl("wload", [128, 2 * 4096], BF16)
                cosq = sbl("cosq", [128, 1024], F32)
                sinq = sbl("sinq", [128, 1024], F32)
                hash_f = sbl("hash_f", [128, 1024], F32)
                hash_g = sbl("hash_g", [128, 1024], F32)
                hash_tb = sbl("hash_tb", [128, 1024], BF16)
                qr_bf = sbl("qr_bf", [128, 1024], BF16)
                qh_bf = sbl("qh_bf", [128, 1024], BF16)
                sq_bf = sbl("sq_bf", [128, 1024], BF16)
                tmp1r = sbl("tmp1r", [1, 1024], F32)

                sc = Sched(nc, store, "p1")
                t_cos = sc.dma("y", lambda e: e.dma_start(out=cosq[:], in_=cosT_own_in))
                t_sin = sc.dma("y", lambda e: e.dma_start(out=sinq[:], in_=sinT2_own_in))
                xv = xb[:].rearrange("p (n c) -> p n c", n=32)
                t_x = []
                prevc = None
                for i in range(8):
                    tl = sc.dma("y", lambda e, i=i: e.dma_start(
                        out=wstg[:], in_=xt_own[:, 4096 * i:4096 * (i + 1)]),
                        after=[prevc])
                    prevc = sc.add("v", lambda e, i=i: e.tensor_copy(
                        out=xb[:, 4096 * i:4096 * (i + 1)], in_=wstg[:]), after=[tl])
                    t_x.append(prevc)
                wv = wload[:].rearrange("p (n c) -> p n c", n=64)

                t_wq, t_qp, t_rope, t_mm1, t_silu, t_t1, t_mm2, t_z2b, t_th, t_sgn = \
                    ([None] * H for _ in range(10))
                t_nma, t_nca, t_nmb, t_ncb, t_spq, t_sph, t_spn = ([None] * H for _ in range(7))

                t_wc = [None] * H
                for eo in range(h_lim):
                    half = 32 * (eo % 2)
                    t_wq[eo] = sc.dma("y", lambda e, eo=eo: e.dma_start(
                        out=wstg[:], in_=wq_t[eo]),
                        after=[t_wc[eo - 1] if eo else None] + t_x)
                    t_wc[eo] = sc.add("v", lambda e, half=half: e.tensor_copy(
                        out=wload[:, 4096 * (half // 32):4096 * (half // 32) + 4096],
                        in_=wstg[:]),
                        after=[t_wq[eo]] + ([t_qp[eo - 2]] if eo >= 2 else []))

                    def qproj(e, half=half):
                        for hx, ps in ((0, ps_a), (1, ps_b)):
                            for ei in range(32):
                                inst = e.matmul(
                                    ps[:], wv[:, half + ei, :],
                                    xb[:, 1024 * ei + 512 * hx:1024 * ei + 512 * (hx + 1)],
                                    start=(ei == 0), stop=(ei == 31))
                        return inst
                    t_qp[eo] = sc.add("t", qproj,
                                      after=[t_wc[eo], t_rope[eo - 1] if eo else None])

                    def rope(e):
                        e.tensor_mul(out=hash_f[:, 0:512], in0=ps_a[:], in1=cosq[:, 0:512])
                        e.tensor_mul(out=hash_f[:, 512:1024], in0=ps_b[:], in1=cosq[:, 512:1024])
                        for base, src in ((0, ps_a), (512, ps_b)):
                            e.tensor_mul(out=hash_g[0:64, base:base + 512],
                                         in0=src[64:128, :], in1=sinq[0:64, base:base + 512])
                            e.tensor_mul(out=hash_g[64:128, base:base + 512],
                                         in0=src[0:64, :], in1=sinq[64:128, base:base + 512])
                        return e.tensor_add(out=qr_bf[:], in0=hash_f[:], in1=hash_g[:])
                    t_rope[eo] = sc.add("v", rope,
                                        after=[t_qp[eo], t_cos, t_sin,
                                               t_spq[eo - 1] if eo else None,
                                               t_sgn[eo - 1] if eo else None,
                                               t_mm1[eo - 1] if eo else None])

                    def mm1(e, eo=eo):
                        e.matmul(ps_c[:], p1b[:, 128 * eo:128 * (eo + 1)], qr_bf[:, 0:512],
                                 start=True, stop=True)
                        return e.matmul(ps_d[:], p1b[:, 128 * eo:128 * (eo + 1)],
                                        qr_bf[:, 512:1024], start=True, stop=True)
                    t_mm1[eo] = sc.add("t", mm1,
                                       after=[t_rope[eo], t_z2b[eo - 1] if eo else None]
                                       + consts_done)

                    def silu(e, eo=eo):
                        e.activation(out=hash_g[:, 0:512], in_=ps_c[:], func=AF.Silu,
                                     bias=b1_s[:, eo:eo + 1], scale=1.0)
                        return e.activation(out=hash_g[:, 512:1024], in_=ps_d[:], func=AF.Silu,
                                            bias=b1_s[:, eo:eo + 1], scale=1.0)
                    t_silu[eo] = sc.add("s", silu, after=[t_mm1[eo], t_th[eo - 1] if eo else None])

                    def t1(e):
                        e.tensor_add(out=hash_f[:], in0=hash_g[:], in1=qr_bf[:])
                        return e.tensor_copy(out=hash_tb[:], in_=hash_f[:])
                    t_t1[eo] = sc.add("v", t1, after=[t_silu[eo]])

                    def mm2(e, eo=eo):
                        e.matmul(ps_c[:], p2b[:, 128 * eo:128 * (eo + 1)], hash_tb[:, 0:512],
                                 start=True, stop=True)
                        return e.matmul(ps_d[:], p2b[:, 128 * eo:128 * (eo + 1)],
                                        hash_tb[:, 512:1024], start=True, stop=True)
                    t_mm2[eo] = sc.add("t", mm2, after=[t_t1[eo]])

                    def z2b(e, eo=eo):
                        e.activation(out=hash_g[:, 0:512], in_=ps_c[:], func=AF.Identity,
                                     bias=b2_s[:, eo:eo + 1], scale=1.0)
                        return e.activation(out=hash_g[:, 512:1024], in_=ps_d[:], func=AF.Identity,
                                            bias=b2_s[:, eo:eo + 1], scale=1.0)
                    t_z2b[eo] = sc.add("s", z2b, after=[t_mm2[eo]])

                    t_th[eo] = sc.add("v", lambda e: e.tensor_add(
                        out=hash_f[:], in0=hash_g[:], in1=hash_f[:]),
                        after=[t_z2b[eo]])

                    def sgn(e):
                        e.activation(out=qh_bf[:], in_=hash_f[:], func=AF.Sign)
                        return e.activation(out=sq_bf[:], in_=qr_bf[:], func=AF.Square)
                    t_sgn[eo] = sc.add("s", sgn,
                                       after=[t_th[eo], t_sph[eo - 1] if eo else None,
                                              t_nmb[eo - 1] if eo else None])

                    t_nma[eo] = sc.add("t", lambda e: e.matmul(
                        ps_n[:], ones_b[:], sq_bf[:, 0:512], start=True, stop=True),
                        after=[t_sgn[eo], t_nca[eo - 1] if eo else None])
                    t_nca[eo] = sc.add("v", lambda e: e.tensor_copy(
                        out=tmp1r[:, 0:512], in_=ps_n[:]), after=[t_nma[eo]])
                    t_nmb[eo] = sc.add("t", lambda e: e.matmul(
                        ps_n[:], ones_b[:], sq_bf[:, 512:1024], start=True, stop=True),
                        after=[t_nca[eo]])
                    t_ncb[eo] = sc.add("v", lambda e: e.tensor_copy(
                        out=tmp1r[:, 512:1024], in_=ps_n[:]),
                        after=[t_nmb[eo], t_spn[eo - 1] if eo else None])

                    t_spq[eo] = sc.dma("y", lambda e, eo=eo: e.dma_start(
                        out=qr_sp[eo], in_=qr_bf[:]), after=[t_rope[eo]])
                    t_sph[eo] = sc.dma("y", lambda e, eo=eo: e.dma_start(
                        out=qh_sp[eo], in_=qh_bf[:]), after=[t_sgn[eo]])
                    t_spn[eo] = sc.dma("y", lambda e, eo=eo: e.dma_start(
                        out=nrm_sp[eo:eo + 1, :], in_=tmp1r[:]), after=[t_ncb[eo]])

                ph1_dma_done = t_spn[h_lim - 1]
                if debug:
                    for eo in range(h_lim):
                        sc.dma("y", lambda e, eo=eo: e.dma_start(
                            out=d_qr[eo], in_=qr_sp[eo]), after=[t_spq[eo]])
                        sc.dma("y", lambda e, eo=eo: e.dma_start(
                            out=d_qh[eo], in_=qh_sp[eo]), after=[t_sph[eo]])
                    sc.dma("y", lambda e: e.dma_start(out=d_nrm, in_=nrm_sp),
                           after=[t_spn[h_lim - 1]])
                sc.emit("ph1")

        mid = ExitStack()
        octx.enter_context(mid)
        sbm = mk(mid)
        krT = sbm("krT", [128, 8 * 2048], BF16)
        vtok = sbm("vtok", [128, 16 * 1024], BF16)

        if n_phases >= 2:
            with ExitStack() as p2ctx:
                sbl = mk(p2ctx)
                xq = sbl("xq", [128, 32 * 1024], BF16)
                wstg = sbl("wstg2", [128, 4096], F32)
                wload = sbl("wload2", [128, 2 * 4096], BF16)
                cosq = sbl("cosq2", [128, 1024], F32)
                sinq = sbl("sinq2", [128, 1024], F32)
                kr_f = sbl("kr_f", [128, 1024], F32)
                kr_g = sbl("kr_g", [128, 1024], F32)
                sq_bf = sbl("sq_bf2", [128, 1024], BF16)
                vc_bf = sbl("vc_bf", [128, 1024], BF16)
                tmp1r = sbl("tmp1r2", [1, 512], F32)

                sc = Sched(nc, store, "p2")
                xv = xq[:].rearrange("p (n c) -> p n c", n=32)
                wv = wload[:].rearrange("p (n c) -> p n c", n=64)

                t_wt = {}
                t_pj = {}
                t_done = {}
                t_k2w = {}
                pj_hist = []
                done_hist = []
                prev_ropevac = None
                prev_sq = None
                prev_nm = [None, None]
                prev_vtr = None
                prev_vevac = None
                t_cos = t_sin = None
                t_wcast = {}
                for half in range(2):
                    t_x = []
                    xdep = [t for k, t in t_pj.items()]
                    rope_dep = [t for k, t in t_done.items() if k[1] < 8]
                    t_cos = sc.dma("y", lambda e, half=half: e.dma_start(
                        out=cosq[:], in_=cosT_kv_in[:, 1024 * half:1024 * (half + 1)]),
                        after=rope_dep)
                    t_sin = sc.dma("y", lambda e, half=half: e.dma_start(
                        out=sinq[:], in_=sinT2_kv_in[:, 1024 * half:1024 * (half + 1)]),
                        after=rope_dep)
                    prevc = t_wcast.get((1 - half, 15))
                    for i in range(8):
                        tl = sc.dma("y", lambda e, i=i, half=half: e.dma_start(
                            out=wstg[:], in_=xt_kv[half, :, 4096 * i:4096 * (i + 1)]),
                            after=[prevc] + xdep)
                        prevc = sc.add("v", lambda e, i=i: e.tensor_copy(
                            out=xq[:, 4096 * i:4096 * (i + 1)], in_=wstg[:]), after=[tl])
                        t_x.append(prevc)
                    for eo in range(16):
                        isk = eo < 8
                        W, col = (wk_t, eo) if isk else (wv_t, eo - 8)
                        wslot = 32 * (eo % 2)
                        t_wt[(half, eo)] = sc.dma(
                            "y", lambda e, W=W, col=col: e.dma_start(
                                out=wstg[:], in_=W[col]),
                            after=[t_wcast.get((half, eo - 1)) if eo else
                                   (t_wcast.get((1 - half, 15)) if half else None)] + t_x)
                        t_wcast[(half, eo)] = sc.add(
                            "v", lambda e, wslot=wslot: e.tensor_copy(
                                out=wload[:, 128 * wslot:128 * wslot + 4096], in_=wstg[:]),
                            after=[t_wt[(half, eo)],
                                   pj_hist[-2] if len(pj_hist) >= 2 else None])

                        def proj(e, wslot=wslot):
                            for hx, ps in ((0, ps_a), (1, ps_b)):
                                for ei in range(32):
                                    inst = e.matmul(
                                        ps[:], wv[:, wslot + ei, :],
                                        xq[:, 1024 * ei + 512 * hx:1024 * ei + 512 * (hx + 1)],
                                        start=(ei == 0), stop=(ei == 31))
                            return inst
                        t_pj[(half, eo)] = sc.add("t", proj,
                                                  after=[t_wcast[(half, eo)],
                                                         done_hist[-1] if done_hist else None]
                                                  + t_x)
                        pj_hist.append(t_pj[(half, eo)])

                        if isk:
                            def ropek_c(e, eo=eo, half=half):
                                cs = cosq[:]
                                sn = sinq[:]
                                e.tensor_mul(out=kr_f[:, 0:512], in0=ps_a[:], in1=cs[:, 0:512])
                                e.tensor_mul(out=kr_f[:, 512:1024], in0=ps_b[:], in1=cs[:, 512:1024])
                                for base, src in ((0, ps_a), (512, ps_b)):
                                    e.tensor_mul(out=kr_g[0:64, base:base + 512],
                                                 in0=src[64:128, :], in1=sn[0:64, base:base + 512])
                                    e.tensor_mul(out=kr_g[64:128, base:base + 512],
                                                 in0=src[0:64, :], in1=sn[64:128, base:base + 512])
                                e.tensor_add(out=kr_f[:], in0=kr_f[:], in1=kr_g[:])
                                return e.tensor_copy(
                                    out=krT[:, 2048 * eo + 1024 * half:2048 * eo + 1024 * (half + 1)],
                                    in_=kr_f[:])
                            tv = sc.add("v", ropek_c,
                                        after=[t_pj[(half, eo)], t_cos, t_sin, prev_sq])
                            t_done[(half, eo)] = tv
                            done_hist.append(tv)
                            ts_ = sc.add("s", lambda e: e.activation(
                                out=sq_bf[:], in_=kr_f[:], func=AF.Square),
                                after=[tv, prev_nm[1]])
                            prev_sq = ts_
                            na = sc.add("t", lambda e: e.matmul(
                                ps_n[:], ones_b[:], sq_bf[:, 0:512], start=True, stop=True),
                                after=[ts_, prev_nm[0]])
                            ca = sc.add("v", lambda e: e.tensor_copy(
                                out=tmp1r[:], in_=ps_n[:]), after=[na])
                            da = sc.dma("y", lambda e, eo=eo, half=half: e.dma_start(
                                out=k2_sp[eo:eo + 1, 1024 * half:1024 * half + 512], in_=tmp1r[:]),
                                after=[ca])
                            nb = sc.add("t", lambda e: e.matmul(
                                ps_n[:], ones_b[:], sq_bf[:, 512:1024], start=True, stop=True),
                                after=[ca])
                            prev_nm[0] = nb
                            cb = sc.add("v", lambda e: e.tensor_copy(
                                out=tmp1r[:], in_=ps_n[:]),
                                after=[nb, da])
                            prev_nm[1] = cb
                            t_k2w[(half, eo)] = sc.dma("y", lambda e, eo=eo, half=half: e.dma_start(
                                out=k2_sp[eo:eo + 1, 1024 * half + 512:1024 * (half + 1)], in_=tmp1r[:]),
                                after=[cb])
                        else:
                            def vevac(e):
                                e.tensor_copy(out=vc_bf[:, 0:512], in_=ps_a[:])
                                return e.tensor_copy(out=vc_bf[:, 512:1024], in_=ps_b[:])
                            tv = sc.add("v", vevac, after=[t_pj[(half, eo)], prev_vtr])
                            t_done[(half, eo)] = tv
                            done_hist.append(tv)

                            def vtr(e):
                                for k in range(8):
                                    inst = e.transpose(
                                        ps_t[:, 128 * k:128 * (k + 1)],
                                        vc_bf[:, 128 * k:128 * (k + 1)], ident_b[:])
                                return inst
                            ttr = sc.add("t", vtr, after=[tv, prev_vevac] + consts_done)
                            prev_vtr = ttr

                            def vout(e, eo=eo, half=half):
                                vview = vtok[:].rearrange("p (tb c) -> p tb c", tb=16)
                                pview = ps_t[:].rearrange("p (tb c) -> p tb c", tb=8)
                                return e.tensor_copy(
                                    out=vview[:, 8 * half:8 * (half + 1),
                                              128 * (eo - 8):128 * (eo - 7)],
                                    in_=pview[:])
                            tvo = sc.add("v", vout, after=[ttr])
                            prev_vevac = tvo

                last_k2 = [t_k2w[(1, eo)] for eo in range(8)]
                if debug:
                    sc.dma("y", lambda e: e.dma_start(out=d_krT, in_=krT[:]),
                           after=[t_done[(h, eo)] for h in range(2) for eo in range(8)])
                    sc.dma("y", lambda e: e.dma_start(out=d_v, in_=vtok[:]),
                           after=[prev_vevac])
                sc.emit("ph2")

            with ExitStack() as p2bctx:
                sbl = mk(p2bctx)
                k2r = sbl("k2r", [8, 2048], F32)
                k2m = sbl("k2m", [8, 1], F32)
                k2mb = sbl("k2mb", [8, 1], BF16)
                k2h = sbl("k2h", [32, 1], F32)
                nrm2r = sbl("nrm2r", [32, 1024], F32)
                bm_b = sbl("bm_b", [32, 1024], BF16)

                sc = Sched(nc, store, "p2b")
                tk2 = sc.dma("y", lambda e: e.dma_start(out=k2r[:], in_=k2_sp),
                             after=last_k2)
                t_nrm = sc.dma("y", lambda e: e.dma_start(out=nrm2r[:], in_=nrm_sp),
                               after=[ph1_dma_done])
                t_r1 = sc.add("v", lambda e: e.reduce_max(k2m[:], k2r[:], axis=AX.X),
                              after=[tk2])
                t_r2 = sc.add("v", lambda e: e.tensor_copy(out=k2mb[:], in_=k2m[:]),
                              after=[t_r1])
                t_r3 = sc.add("t", lambda e: e.matmul(
                    ps_s[0:32, 0:1], exp832_b[:], k2mb[:], start=True, stop=True),
                    after=[t_r2] + consts_done)
                t_r4 = sc.add("v", lambda e: e.tensor_copy(out=k2h[:], in_=ps_s[0:32, 0:1]),
                              after=[t_r3])
                t_r6 = sc.add("v", lambda e: e.tensor_scalar(
                    out=bm_b[:], in0=nrm2r[:], scalar1=k2h[:], scalar2=-0.5 * INV_SQRT_D,
                    op0=ALU.add, op1=ALU.mult), after=[t_r4, t_nrm])
                t_bj = []
                for j in range(NB):
                    ta = sc.add("t", lambda e, j=j: e.transpose(
                        ps_t[:, 0:32], bm_b[:, 128 * j:128 * (j + 1)], ident_b[0:32, 0:32]),
                        after=[t_r6] + ([t_bj[-1]] if t_bj else [t_r4]))
                    tb = sc.add("v", lambda e, j=j: e.tensor_copy(
                        out=biasT[:, 32 * j:32 * (j + 1)], in_=ps_t[:, 0:32]), after=[ta])
                    t_bj.append(tb)
                if debug:
                    sc.dma("y", lambda e: e.dma_start(out=d_k2, in_=k2h[:]), after=[t_r4])
                    sc.dma("y", lambda e: e.dma_start(out=d_k2r, in_=k2r[:]), after=[t_r1])
                    sc.dma("y", lambda e: e.dma_start(out=d_k2m[0], in_=k2m[:]), after=[t_r2])
                    sc.dma("y", lambda e: e.dma_start(out=d_biasT, in_=biasT[:]),
                           after=t_bj)
                sc.emit("ph2b")

        if n_phases >= 3:
            with ExitStack() as p3ctx:
                sbl = mk(p3ctx)
                khT = sbl("khT", [128, 4 * 2048], BF16)
                simt = sbl("simt", [128, 4 * 2048], BF16)
                qrg = sbl("qrg", [128, 2 * 4 * 128], BF16)
                qhg = sbl("qhg", [128, 2 * 4 * 128], BF16)
                cum_f = sbl("cum_f", [128, 2048], F32)
                eqs_b = sbl("eqs_b", [128, 2048], BF16)
                m01_b = sbl("m01_b", [128, 2048], BF16)
                praw_b = sbl("praw_b", [128, 2048], BF16)
                ptr_b = sbl("ptr_b", [128, 2048], BF16)
                o_hg = sbl("o_hg", [128, 4 * 128], BF16)
                scr_bf = sbl("scr_bf3", [128, 2048], BF16)
                st_f = sbl("st_f", [128, 64], F32)
                st_u = sbl("st_u", [128, 8], U8)
                zcol = sbl("zcol", [128, 8], F32)

                sc = Sched(nc, store, "p3")

                def chunks_of(w):
                    out = []
                    c0 = 0
                    while c0 < w:
                        cw = min(512, w - c0)
                        out.append((c0, cw))
                        c0 += cw
                    return out

                prev_kh = {}
                prev_ps4 = None
                prev_cum = None
                prev_eqs = None
                prev_m_items = []
                prev_oev = [None] * 4
                prev_osp = None
                prev_pst = None
                prev_pss = None
                prev_ql = [None, None]
                prev_simuse = []
                m4_hist = []
                av_hist = []
                last_khT_reader = [None] * 4

                for hg in range(8):
                    g = hg
                    kh_done = []
                    for i in range(4):
                        h = 4 * hg + i

                        def kh1(e, h=h, g=g):
                            for c in range(4):
                                inst = e.matmul(ps4[c][:], p1b[:, 128 * h:128 * (h + 1)],
                                                krT[:, 2048 * g + 512 * c:2048 * g + 512 * (c + 1)],
                                                start=True, stop=True)
                            return inst
                        tk1 = sc.add("t", kh1, after=[prev_ps4, last_khT_reader[i]] + kh_done)

                        def kh2(e, h=h):
                            for c in range(4):
                                inst = e.activation(out=cum_f[:, 512 * c:512 * (c + 1)],
                                                    in_=ps4[c][:], func=AF.Silu,
                                                    bias=b1_s[:, h:h + 1], scale=1.0)
                            return inst
                        tk2 = sc.add("s", kh2, after=[tk1, prev_cum])

                        def kh3(e, g=g):
                            e.tensor_add(out=cum_f[:], in0=cum_f[:],
                                         in1=krT[:, 2048 * g:2048 * (g + 1)])
                            return e.tensor_copy(out=eqs_b[:], in_=cum_f[:])
                        tk3 = sc.add("v", kh3, after=[tk2, prev_eqs])

                        def kh4(e, h=h):
                            for c in range(4):
                                inst = e.matmul(ps4[c][:], p2b[:, 128 * h:128 * (h + 1)],
                                                eqs_b[:, 512 * c:512 * (c + 1)],
                                                start=True, stop=True)
                            return inst
                        tk4 = sc.add("t", kh4, after=[tk3, tk2])

                        def kh5(e, h=h):
                            e.tensor_scalar(out=cum_f[:], in0=cum_f[:], scalar1=b2_s[:, h:h + 1],
                                            scalar2=None, op0=ALU.add)
                            for c in range(4):
                                inst = e.tensor_add(out=cum_f[:, 512 * c:512 * (c + 1)],
                                                    in0=cum_f[:, 512 * c:512 * (c + 1)],
                                                    in1=ps4[c][:])
                            return inst
                        tk5 = sc.add("v", kh5, after=[tk4])

                        tk6 = sc.add("s", lambda e, i=i: e.activation(
                            out=khT[:, 2048 * i:2048 * (i + 1)], in_=cum_f[:], func=AF.Sign),
                            after=[tk5, last_khT_reader[i]])
                        kh_done.append(tk6)
                        prev_ps4 = tk5
                        prev_cum = tk6
                        prev_eqs = tk4
                        prev_kh[i] = tk6

                    if debug:
                        for i in range(4):
                            sc.dma("y", lambda e, i=i, hg=hg: e.dma_start(
                                out=d_kh[4 * hg + i], in_=khT[:, 2048 * i:2048 * (i + 1)]),
                                after=[kh_done[i]])

                    prev_simuse = [None] * 4
                    for j in range(NB):
                        w = wj(j)
                        par = j % 2
                        qsl = 512 * par
                        ch = chunks_of(w)
                        srch = j >= 1

                        tql = sc.dma("y", lambda e, hg=hg, j=j, qsl=qsl: e.dma_start(
                            out=qrg[:, qsl:qsl + 512].rearrange("p (n c) -> p n c", n=4),
                            in_=qr_sp[4 * hg:4 * hg + 4, :, 128 * j:128 * (j + 1)]
                            .rearrange("n p c -> p n c")), after=[prev_ql[par], ph1_dma_done])
                        tqh = sc.dma("y", lambda e, hg=hg, j=j, qsl=qsl: e.dma_start(
                            out=qhg[:, qsl:qsl + 512].rearrange("p (n c) -> p n c", n=4),
                            in_=qh_sp[4 * hg:4 * hg + 4, :, 128 * j:128 * (j + 1)]
                            .rearrange("n p c -> p n c")), after=[prev_ql[par], ph1_dma_done])

                        sev = [None] * 4
                        if srch:
                            for i in range(4):
                                def simmm(e, i=i, qsl=qsl, ch=ch):
                                    for c, (c0, cw) in enumerate(ch):
                                        inst = e.matmul(
                                            ps4[c % 4][:, 0:cw],
                                            qhg[:, qsl + 128 * i:qsl + 128 * (i + 1)],
                                            khT[:, 2048 * i + c0:2048 * i + c0 + cw],
                                            start=True, stop=True)
                                    return inst
                                tsm = sc.add("t", simmm,
                                             after=[tqh, prev_ps4, kh_done[i]])
                                last_khT_reader[i] = tsm

                                def simev(e, i=i, j=j, ch=ch, w=w):
                                    for (c0, cw) in ch:
                                        e.tensor_copy(
                                            out=simt[:, 2048 * i + c0:2048 * i + c0 + cw],
                                            in_=ps4[(c0 // 512) % 4][:, 0:cw])
                                    return e.copy_predicated(
                                        out=simt[:, 2048 * i + w - 256:2048 * i + w],
                                        mask=minv_s[:, 256 * j:256 * (j + 1)],
                                        data=neg512[:])
                                sev[i] = sc.add("v", simev, after=[tsm] + prev_simuse)
                                prev_ps4 = sev[i]

                        m_items = []
                        if srch:
                            t0 = sc.add("v", lambda e: e.memset(st_f[:, 0:4], -128.0),
                                        after=prev_m_items + sev)
                            prev_upd2 = None
                            prev_cs = t0
                            prev_cv = t0
                            last_upd3 = t0
                            for it, k in enumerate(range(8, -1, -1)):
                                step = float(2 ** k)

                                def sa(e, step=step):
                                    e.tensor_scalar(out=st_f[:, 4:8], in0=st_f[:, 0:4],
                                                    scalar1=step, scalar2=None, op0=ALU.add)
                                    e.tensor_scalar(out=st_f[:, 8:12], in0=st_f[:, 0:4],
                                                    scalar1=step - 0.5, scalar2=None, op0=ALU.add)
                                    return e.tensor_scalar(
                                        out=st_f[:, 12:16], in0=st_f[:, 0:4],
                                        scalar1=-1.0, scalar2=-(step - 0.5),
                                        op0=ALU.mult, op1=ALU.add)
                                tsa = sc.add("v", sa, after=[last_upd3])

                                def cs(e, w=w):
                                    for i in range(2):
                                        inst = e.activation(
                                            out=scr_bf[:, 0:w], in_=simt[:, 2048 * i:2048 * i + w],
                                            func=AF.Sign, bias=st_f[:, 12 + i:13 + i], scale=1.0,
                                            accum_out=st_f[:, 16 + i:17 + i])
                                    return inst
                                tcs = sc.add("s", cs, after=[tsa, prev_upd2, prev_cs])
                                prev_cs = tcs

                                def cv(e, w=w):
                                    for i in range(2, 4):
                                        inst = e.tensor_scalar(
                                            out=praw_b[:, 0:w], in0=simt[:, 2048 * i:2048 * i + w],
                                            scalar1=st_f[:, 8 + i:9 + i], scalar2=None,
                                            op0=ALU.is_ge, op1=ALU.add,
                                            accum_out=st_f[:, 16 + i:17 + i])
                                    return inst
                                tcv = sc.add("v", cv, after=[tsa, prev_upd2, prev_cv])
                                prev_cv = tcv

                                tu1 = sc.add("v", lambda e, w=w: e.tensor_scalar(
                                    out=st_f[:, 16:18], in0=st_f[:, 16:18],
                                    scalar1=0.5, scalar2=w * 0.5,
                                    op0=ALU.mult, op1=ALU.add), after=[tcs])
                                tu2 = sc.add("v", lambda e: e.tensor_scalar(
                                    out=st_u[:, 0:4], in0=st_f[:, 16:20],
                                    scalar1=float(NUM_REMAIN), scalar2=None,
                                    op0=ALU.is_ge), after=[tu1, tcv])
                                prev_upd2 = tu2
                                tu3 = sc.add("v", lambda e: e.copy_predicated(
                                    out=st_f[:, 0:4], mask=st_u[:, 0:4],
                                    data=st_f[:, 4:8]), after=[tu2])
                                last_upd3 = tu3

                            def fin1(e):
                                e.tensor_scalar(out=st_f[:, 8:12], in0=st_f[:, 0:4],
                                                scalar1=0.5, scalar2=None, op0=ALU.add)
                                return e.tensor_scalar(out=st_f[:, 12:16], in0=st_f[:, 0:4],
                                                       scalar1=-1.0, scalar2=-0.5,
                                                       op0=ALU.mult, op1=ALU.add)
                            tf1 = sc.add("v", fin1, after=[last_upd3])
                            tf2 = sc.add("s", lambda e, w=w: [
                                e.activation(out=scr_bf[:, 0:w], in_=simt[:, 2048 * i:2048 * i + w],
                                             func=AF.Sign, bias=st_f[:, 12 + i:13 + i], scale=1.0,
                                             accum_out=st_f[:, 16 + i:17 + i])
                                for i in range(2)][-1], after=[tf1, prev_cs])
                            tf3 = sc.add("v", lambda e, w=w: [
                                e.tensor_scalar(out=praw_b[:, 0:w],
                                                in0=simt[:, 2048 * i:2048 * i + w],
                                                scalar1=st_f[:, 8 + i:9 + i], scalar2=None,
                                                op0=ALU.is_ge, op1=ALU.add,
                                                accum_out=st_f[:, 16 + i:17 + i])
                                for i in range(2, 4)][-1], after=[tf1, prev_cv])
                            tf4 = sc.add("v", lambda e, w=w: e.tensor_scalar(
                                out=st_f[:, 16:18], in0=st_f[:, 16:18],
                                scalar1=0.5, scalar2=w * 0.5,
                                op0=ALU.mult, op1=ALU.add), after=[tf2])
                            tf5 = sc.add("v", lambda e: e.tensor_scalar(
                                out=st_f[:, 20:24], in0=st_f[:, 16:20],
                                scalar1=-1.0, scalar2=float(NUM_REMAIN),
                                op0=ALU.mult, op1=ALU.add), after=[tf4, tf3])
                            tf6 = sc.add("v", lambda e: e.tensor_scalar(
                                out=st_f[:, 24:28], in0=st_f[:, 20:24],
                                scalar1=float(2 ** -12), scalar2=None,
                                op0=ALU.mult), after=[tf5])
                            tf7 = sc.add("v", lambda e: e.tensor_sub(
                                out=st_f[:, 28:32], in0=st_f[:, 0:4], in1=st_f[:, 24:28]),
                                after=[tf6])
                            if debug:
                                sc.dma("y", lambda e, hg=hg, j=j: e.dma_start(
                                    out=d_tau[j, :, 4 * hg:4 * hg + 4], in_=st_f[:, 0:4]),
                                    after=[tf7])
                                sc.dma("y", lambda e, hg=hg, j=j: e.dma_start(
                                    out=d_rr[j, :, 4 * hg:4 * hg + 4], in_=st_f[:, 20:24]),
                                    after=[tf7])
                        else:
                            tf7 = None
                            tf3 = None

                        oev = [None] * 4
                        new_simuse = [None] * 4
                        prev_exp = None
                        for i in range(4):
                            prev_m4 = m4_hist[-1] if m4_hist else None
                            prev_avlast = av_hist[-1] if av_hist else None
                            h = 4 * hg + i
                            if srch:
                                tm1 = sc.add("v", lambda e, i=i, w=w: e.tensor_scalar(
                                    out=eqs_b[:, 0:w], in0=simt[:, 2048 * i:2048 * i + w],
                                    scalar1=st_f[:, 0 + i:1 + i], scalar2=float(2 ** -12),
                                    op0=ALU.is_equal, op1=ALU.mult),
                                    after=[tf7, prev_eqs, prev_m4])
                                tm2 = sc.add("v", lambda e, i=i, w=w: e.tensor_tensor_scan(
                                    out=cum_f[:, 0:w], data0=eqs_b[:, 0:w],
                                    data1=zeros_b[:, 0:w], initial=st_f[:, 28 + i:29 + i],
                                    op0=ALU.add, op1=ALU.add),
                                    after=[tm1, prev_cum])
                                tm3 = sc.add("v", lambda e, i=i, w=w: e.tensor_tensor(
                                    out=m01_b[:, 0:w], in0=simt[:, 2048 * i:2048 * i + w],
                                    in1=cum_f[:, 0:w], op=ALU.is_ge),
                                    after=[tm2, prev_m4])
                                m_items += [tm1, tm2]
                                prev_cum = tm3
                                prev_eqs = tm2
                                mask_ap = m01_b
                            else:
                                tm3 = None
                                mask_ap = m01j0

                            def scr(e, i=i, qsl=qsl, g=g, ch=ch):
                                for c, (c0, cw) in enumerate(ch):
                                    inst = e.matmul(
                                        ps4[c % 4][:, 0:cw],
                                        qrg[:, qsl + 128 * i:qsl + 128 * (i + 1)],
                                        krT[:, 2048 * g + c0:2048 * g + c0 + cw],
                                        start=True, stop=True)
                                return inst
                            tscr = sc.add("t", scr, after=[tql, prev_ps4])

                            def expf(e, h=h, j=j, ch=ch):
                                for c, (c0, cw) in enumerate(ch):
                                    inst = e.activation(
                                        out=praw_b[:, c0:c0 + cw], in_=ps4[c % 4][:, 0:cw],
                                        func=AF.Exp, scale=INV_SQRT_D,
                                        bias=biasT[:, 32 * j + h:32 * j + h + 1])
                                return inst
                            texp = sc.add("s", expf,
                                          after=[tscr, tf3, prev_exp, prev_m4])
                            prev_exp = texp
                            prev_ps4 = texp

                            if debug and hg == 0 and j == 3 and i == 0:
                                sc.dma("y", lambda e: e.dma_start(out=d_praw[:, 0:1024], in_=praw_b[:, 0:1024]),
                                       after=[texp])
                                sc.dma("y", lambda e: e.dma_start(out=d_m01[:, 0:1024], in_=m01_b[:, 0:1024]),
                                       after=[tm3]) if tm3 else None
                            tm4 = sc.add("v", lambda e, i=i, w=w, mask_ap=mask_ap: e.scalar_tensor_tensor(
                                out=simt[:, 2048 * i:2048 * i + w], in0=praw_b[:, 0:w],
                                scalar=1.0, in1=mask_ap[:, 0:w],
                                op0=ALU.mult, op1=ALU.mult,
                                accum_out=zcol[:, i:i + 1]),
                                after=[texp, tm3, prev_oev[i], sev[i]])
                            if debug and hg == 0 and j == 3 and i == 0:
                                sc.dma("y", lambda e: e.dma_start(out=d_pp[:, 0:1024], in_=simt[:, 0:1024]),
                                       after=[tm4])
                            tr1 = sc.add("v", lambda e, i=i: e.reciprocal(
                                out=zcol[:, 4 + i:5 + i], in_=zcol[:, i:i + 1]),
                                after=[tm4])
                            tm5 = sc.add("v", lambda e, i=i, w=w: e.tensor_scalar(
                                out=simt[:, 2048 * i:2048 * i + w],
                                in0=simt[:, 2048 * i:2048 * i + w],
                                scalar1=zcol[:, 4 + i:5 + i], scalar2=None,
                                op0=ALU.mult), after=[tr1])
                            if debug and hg == 0 and j == 3 and i == 0:
                                sc.dma("y", lambda e: e.dma_start(out=d_z, in_=zcol[:]), after=[tr1])

                            ntile = w // 128
                            avdeps = []
                            av_first = True
                            tav = None
                            for wv0 in range(0, ntile, 8):
                                nw = min(8, ntile - wv0)

                                def trp(e, i=i, wv0=wv0, nw=nw):
                                    for t_ in range(nw):
                                        inst = e.transpose(
                                            ps_t[:, 128 * t_:128 * (t_ + 1)],
                                            simt[:, 2048 * i + 128 * (wv0 + t_):
                                                 2048 * i + 128 * (wv0 + t_ + 1)],
                                            ident_b[:])
                                    return inst
                                ttrp = sc.add("t", trp, after=[tm5, prev_pst])

                                def pev(e, wv0=wv0, nw=nw):
                                    return e.tensor_copy(
                                        out=ptr_b[:, 128 * wv0:128 * (wv0 + nw)],
                                        in_=ps_t[:, 0:128 * nw])
                                tpev = sc.add("v", pev, after=[ttrp, prev_avlast])
                                prev_pst = tpev

                                def av(e, g=g, wv0=wv0, nw=nw, av_first=av_first,
                                       last=(wv0 + 8 >= ntile)):
                                    vv = vtok[:].rearrange("p (tb c) -> p tb c", tb=16)
                                    for t_ in range(nw):
                                        tb = wv0 + t_
                                        inst = e.matmul(
                                            ps_s[:], vv[:, tb, 128 * g:128 * (g + 1)],
                                            ptr_b[:, 128 * tb:128 * (tb + 1)],
                                            start=(av_first and t_ == 0),
                                            stop=(last and t_ == nw - 1))
                                    return inst
                                tav = sc.add("t", av, after=[tpev, prev_pss])
                                av_first = False

                            oev[i] = sc.add("v", lambda e, i=i: e.tensor_copy(
                                out=o_hg[:, 128 * i:128 * (i + 1)], in_=ps_s[:]),
                                after=[tav])
                            prev_pss = oev[i]
                            m4_hist.append(tm5)
                            av_hist.append(tav)
                            new_simuse[i] = tm5

                        prev_simuse = new_simuse
                        prev_m_items = m_items
                        prev_ql[par] = sc.dma("y", lambda e, hg=hg, j=j: e.dma_start(
                            out=o_sp[4 * hg:4 * hg + 4, :, 128 * j:128 * (j + 1)]
                            .rearrange("n p c -> p n c"),
                            in_=o_hg[:].rearrange("p (n c) -> p n c", n=4)),
                            after=oev)
                        prev_osp = prev_ql[par]
                        prev_oev = [prev_osp] * 4

                ph3_dma_done = prev_osp
                if debug:
                    for h in range(H):
                        sc.dma("y", lambda e, h=h: e.dma_start(
                            out=d_o[h], in_=o_sp[h]), after=[prev_osp])
                sc.emit("ph3")

        mid.close()

        def ln_block(sc, tld, ln_x, ln_y, ln_c, deps_sq=None):
            ts1 = sc.add("v", lambda e: e.reduce_sum(
                ln_c[:, 0:1], ln_x[:], axis=AX.X), after=[tld])
            ts2 = sc.add("s", lambda e: e.activation(
                out=ln_y[:], in_=ln_x[:], func=AF.Square,
                accum_out=ln_c[:, 1:2]), after=[tld] + (deps_sq or []))
            ts3 = sc.add("v", lambda e: e.tensor_scalar(
                out=ln_c[:, 2:3], in0=ln_c[:, 0:1], scalar1=1.0 / 4096, scalar2=None,
                op0=ALU.mult), after=[ts1])
            ts4 = sc.add("v", lambda e: e.tensor_scalar(
                out=ln_c[:, 3:4], in0=ln_c[:, 1:2], scalar1=1.0 / 4096, scalar2=None,
                op0=ALU.mult), after=[ts2])
            ts5 = sc.add("v", lambda e: e.tensor_mul(
                out=ln_c[:, 4:5], in0=ln_c[:, 2:3], in1=ln_c[:, 2:3]), after=[ts3])
            ts6 = sc.add("v", lambda e: e.tensor_sub(
                out=ln_c[:, 5:6], in0=ln_c[:, 3:4], in1=ln_c[:, 4:5]), after=[ts4, ts5])
            ts7 = sc.add("v", lambda e: e.tensor_scalar(
                out=ln_c[:, 6:7], in0=ln_c[:, 5:6], scalar1=EPS, scalar2=None,
                op0=ALU.add), after=[ts6])
            ts8 = sc.add("v", lambda e: e.reciprocal(
                out=ln_c[:, 7:8], in_=ln_c[:, 6:7]), after=[ts7])
            ts9 = sc.add("s", lambda e: e.activation(
                out=ln_c[:, 8:9], in_=ln_c[:, 7:8], func=AF.Sqrt), after=[ts8])
            ta1 = sc.add("v", lambda e: e.tensor_scalar(
                out=ln_y[:], in0=ln_x[:], scalar1=ln_c[:, 2:3], scalar2=ln_c[:, 8:9],
                op0=ALU.subtract, op1=ALU.mult), after=[ts9, ts2])
            return ta1

        def stream_matmul_phase(sc, Wdram, lhs_tiles, resid_sp, out_sp, wload,
                                extra_mm=None, extra_deps=None, resid_ready=None):
            wv = wload[:].rearrange("p (n c) -> p n c", n=32)
            wstg = stream_bufs["wstg"]
            xrc = stream_bufs["xrc"]
            yc = stream_bufs["yc"]
            t_w = [None] * 8
            t_wc = [None] * 8
            t_mm = {}
            t_ev = {}
            t_dma = {}
            last_ev_par = [None, None]
            last_dma_par = [None, None]
            for c in range(8):
                t_w[c] = sc.dma("y", lambda e, c=c: e.dma_start(
                    out=wstg[:], in_=Wdram[c]),
                    after=[t_wc[c - 1] if c else None] + stream_bufs.get("wstg_war", []))
                t_wc[c] = sc.add("v", lambda e: e.tensor_copy(out=wload[:], in_=wstg[:]),
                                 after=[t_w[c], t_mm.get((c - 1, NB - 1))])
                for j in range(NB):
                    par = j % 2
                    t_xr = sc.dma("y", lambda e, c=c, j=j, par=par: e.dma_start(
                        out=xrc[:, 512 * par:512 * (par + 1)],
                        in_=resid_sp[j][:, 512 * c:512 * (c + 1)]),
                        after=[last_ev_par[par], resid_ready])

                    def mmf(e, c=c, j=j):
                        ps = ps_a if j % 2 == 0 else ps_b
                        last = extra_mm is None
                        for ei in range(32):
                            inst = e.matmul(ps[:], lhs_tiles(ei, j),
                                            wv[:, ei, :],
                                            start=(ei == 0), stop=(last and ei == 31))
                        if extra_mm is not None:
                            inst = extra_mm(e, ps, c, j)
                        return inst
                    t_mm[(c, j)] = sc.add("t", mmf,
                                          after=[t_wc[c], last_ev_par[par]] + (extra_deps(j) if extra_deps else []))

                    def ev(e, c=c, j=j, par=par):
                        ps = ps_a if j % 2 == 0 else ps_b
                        return e.tensor_add(out=yc[:, 512 * par:512 * (par + 1)],
                                            in0=ps[:], in1=xrc[:, 512 * par:512 * (par + 1)])
                    t_ev[(c, j)] = sc.add("v", ev,
                                          after=[t_mm[(c, j)], t_xr, last_dma_par[par]])
                    last_ev_par[par] = t_ev[(c, j)]
                    t_dma[(c, j)] = sc.dma("y", lambda e, c=c, j=j, par=par: e.dma_start(
                        out=out_sp[j][:, 512 * c:512 * (c + 1)],
                        in_=yc[:, 512 * par:512 * (par + 1)]), after=[t_ev[(c, j)]])
                    last_dma_par[par] = t_dma[(c, j)]
            return t_dma

        if n_phases >= 4:
            with ExitStack() as p4ctx:
                sbl = mk(p4ctx)
                o_all = sbl("o_all", [128, NB * 4096], BF16)
                wload = sbl("wload4", [128, 16384], BF16)
                stream_bufs = {"wstg": sbl("wstg4", [128, 16384], F32),
                               "xrc": sbl("xrc", [128, 2 * 512], F32),
                               "yc": sbl("y1c", [128, 2 * 512], F32)}

                sc = Sched(nc, store, "p4a")
                ov = o_all[:].rearrange("p (j n c) -> p j n c", j=NB, n=32)
                t_ol = [None] * NB
                for j in range(NB):
                    t_ol[j] = sc.dma("y", lambda e, j=j: e.dma_start(
                        out=ov[:, j, :, :],
                        in_=o_sp[:, :, 128 * j:128 * (j + 1)].rearrange("n p c -> p n c")),
                        after=[ph3_dma_done])
                wo_dma = stream_matmul_phase(
                    sc, wo_t, lambda ei, j: ov[:, j, ei, :], x_own, x2_sp, wload,
                    extra_deps=lambda j: [t_ol[j]])
                sc.emit("ph4a")

        mid2 = ExitStack()
        octx.enter_context(mid2)
        x2Tb = mk(mid2)("x2Tb", [128, 32 * 1024], BF16)

        if n_phases >= 4:
            with ExitStack() as p4bctx:
                sbl = mk(p4bctx)
                ln_x = sbl("ln_x4", [128, 4096], F32)
                ln_y = sbl("ln_y4", [128, 4096], F32)
                ln_c = sbl("ln_c4", [128, 16], F32)
                x2b = sbl("x2b4", [128, 4096], BF16)

                sc = Sched(nc, store, "p4b")
                prev_tr = None
                prev_st = None
                prev_apply = None
                for j in range(NB):
                    tld = sc.dma("y", lambda e, j=j: e.dma_start(
                        out=ln_x[:], in_=x2_sp[j]),
                        after=[wo_dma[(c, j)] for c in range(8)] + [prev_st, prev_apply])
                    ta1 = ln_block(sc, tld, ln_x, ln_y, ln_c, deps_sq=[prev_st])
                    prev_apply = ta1
                    ta2 = sc.add("v", lambda e: e.tensor_copy(out=x2b[:], in_=ln_y[:]),
                                 after=[ta1, prev_tr])
                    tst = sc.dma("y", lambda e, j=j: e.dma_start(
                        out=x2_sp[j], in_=ln_y[:]), after=[ta1])
                    prev_st = tst
                    if debug:
                        sc.dma("y", lambda e, j=j: e.dma_start(out=d_x2[j], in_=ln_y[:]),
                               after=[ta1])
                    for wv0 in range(0, 32, 8):
                        ttr = sc.add("t", lambda e, wv0=wv0: [
                            e.transpose(ps_t[:, 128 * t_:128 * (t_ + 1)],
                                        x2b[:, 128 * (wv0 + t_):128 * (wv0 + t_ + 1)],
                                        ident_b[:]) for t_ in range(8)][-1],
                            after=[ta2, prev_tr])
                        tpe = sc.add("v", lambda e, wv0=wv0, j=j: e.tensor_copy(
                            out=x2Tb[:].rearrange("p (n c) -> p n c", n=32)
                            [:, wv0:wv0 + 8, 128 * j:128 * (j + 1)],
                            in_=ps_t[:].rearrange("p (n c) -> p n c", n=8)),
                            after=[ttr])
                        prev_tr = tpe
                ph4b_dma_done = prev_st
                sc.emit("ph4b")

        if n_phases >= 5:
            with ExitStack() as p5ctx:
                sbl = mk(p5ctx)
                wload = sbl("wload5", [128, 16384], BF16)
                gwT_b = sbl("gwT_b", [128, 32 * 4], BF16)
                lAT_b = sbl("lAT_b", [128, 32 * 32], BF16)
                lBp_b = sbl("lBp_b", [32, 4096], BF16)
                ebias_s = sbl("ebias_s", [128, 4], F32)
                lg_f = sbl("lg_f", [128, 8], F32)
                oh_f = sbl("oh_f", [128, 8], F32)
                zms = sbl("zms", [128, NB * 32], BF16)
                zmT = sbl("zmT", [32, NB * 128], BF16)
                stream_bufs = {"wstg": sbl("wstg5", [128, 16384], F32),
                               "xrc": sbl("x2rc", [128, 2 * 512], F32),
                               "yc": sbl("y2c", [128, 2 * 512], F32)}
                psdbg = sbl("psdbg", [128, 64], F32)

                sc = Sched(nc, store, "p5")
                wstg5 = stream_bufs["wstg"]
                tl1 = sc.dma("y", lambda e: e.dma_start(out=wstg5[:, 0:128], in_=gwT_in))
                tc1 = sc.add("v", lambda e: e.tensor_copy(out=gwT_b[:], in_=wstg5[:, 0:128]),
                             after=[tl1])
                tl2 = sc.dma("y", lambda e: e.dma_start(out=wstg5[:, 128:1152], in_=loraAT_in))
                tc2 = sc.add("v", lambda e: e.tensor_copy(out=lAT_b[:], in_=wstg5[:, 128:1152]),
                             after=[tl2])
                tl3 = sc.dma("y", lambda e: e.dma_start(out=wstg5[0:32, 1152:5248], in_=loraBp_in))
                tc3 = sc.add("v", lambda e: e.tensor_copy(out=lBp_b[:], in_=wstg5[0:32, 1152:5248]),
                             after=[tl3])
                tc4 = sc.dma("y", lambda e: e.dma_start(out=ebias_s[:], in_=ebias_in))

                if debug:
                    sc.dma("y", lambda e: e.dma_start(out=d_lat[0], in_=lAT_b[:]), after=[tc2])
                x2v = x2Tb[:].rearrange("p (n c) -> p n c", n=32)
                prev_z = None
                prev_tdoh = None
                prev_tdoh2 = None
                t_zmT = [None] * NB
                for j in range(NB):
                    def gmm(e, j=j):
                        for ei in range(32):
                            inst = e.matmul(ps_s[:, 0:4], x2v[:, ei, 128 * j:128 * (j + 1)],
                                            gwT_b[:, 4 * ei:4 * (ei + 1)],
                                            start=(ei == 0), stop=(ei == 31))
                        return inst
                    tg = sc.add("t", gmm, after=[tc1, prev_z])
                    tb1 = sc.add("v", lambda e: e.tensor_add(
                        out=lg_f[:, 0:4], in0=ps_s[:, 0:4], in1=ebias_s[:]),
                        after=[tg, tc4, prev_z])
                    tdlg = sc.dma("y", lambda e, j=j: e.dma_start(
                        out=d_lg[j], in_=lg_f[:]), after=[tb1]) if debug else None
                    tb2 = sc.add("v", lambda e: e.reduce_max(
                        out=lg_f[:, 4:5], in_=lg_f[:, 0:4], axis=AX.X), after=[tb1])
                    tb3 = sc.add("v", lambda e: e.tensor_scalar(
                        out=oh_f[:, 0:4], in0=lg_f[:, 0:4], scalar1=lg_f[:, 4:5],
                        scalar2=None, op0=ALU.is_ge), after=[tb2, prev_tdoh2])
                    tb4 = sc.add("v", lambda e: e.tensor_copy(
                        out=oh_f[:, 4:5], in_=oh_f[:, 0:1]), after=[tb3])
                    tb5s = [tb4]
                    for ee in range(1, 4):
                        tb5a = sc.add("v", lambda e: e.tensor_scalar(
                            out=oh_f[:, 7:8], in0=oh_f[:, 4:5], scalar1=-1.0, scalar2=1.0,
                            op0=ALU.mult, op1=ALU.add), after=[tb5s[-1]])
                        tb5b = sc.add("v", lambda e, ee=ee: e.tensor_mul(
                            out=oh_f[:, ee:ee + 1], in0=oh_f[:, ee:ee + 1],
                            in1=oh_f[:, 7:8]), after=[tb5a])
                        tb5c = sc.add("v", lambda e, ee=ee: e.tensor_add(
                            out=oh_f[:, 4:5], in0=oh_f[:, 4:5],
                            in1=oh_f[:, ee:ee + 1]), after=[tb5b])
                        tb5s.append(tb5c)

                    tdoh = sc.dma("y", lambda e, j=j: e.dma_start(
                        out=d_oh[j], in_=oh_f[:]), after=[tb5s[-1]]) if debug else None

                    def zmm(e, j=j):
                        for ei in range(32):
                            inst = e.matmul(ps_s[:, 0:32], x2v[:, ei, 128 * j:128 * (j + 1)],
                                            lAT_b[:, 32 * ei:32 * (ei + 1)],
                                            start=(ei == 0), stop=(ei == 31))
                        return inst
                    tz = sc.add("t", zmm, after=[tc2, tb1])
                    prev_tdoh2 = tdoh
                    if debug and j == 0:
                        tq1 = sc.add("v", lambda e: e.tensor_copy(
                            out=psdbg[:, 0:32], in_=ps_s[:, 0:32]), after=[tz])
                        tdelay = sc.add("t", lambda e: e.nop(), after=[tq1])
                        tq2 = sc.add("v", lambda e: e.tensor_copy(
                            out=psdbg[:, 32:64], in_=ps_s[:, 0:32]), after=[tdelay])
                        sc.dma("y", lambda e: e.dma_start(out=d_ps, in_=psdbg[:]), after=[tq2])
                    zm_items = []
                    for ee in range(4):
                        zm_items.append(sc.add("v", lambda e, j=j, ee=ee: e.tensor_scalar(
                            out=zms[:, 32 * j + 8 * ee:32 * j + 8 * (ee + 1)],
                            in0=ps_s[:, 8 * ee:8 * (ee + 1)],
                            scalar1=oh_f[:, ee:ee + 1], scalar2=None,
                            op0=ALU.mult), after=[tz, tb5s[-1]]))
                    prev_z = zm_items[-1]
                    ttr = sc.add("t", lambda e, j=j: e.transpose(
                        ps_t[0:32, 0:128], zms[:, 32 * j:32 * (j + 1)], ident_b[:]),
                        after=zm_items + [t_zmT[j - 1] if j else None])
                    t_zmT[j] = sc.add("v", lambda e, j=j: e.tensor_copy(
                        out=zmT[:, 128 * j:128 * (j + 1)], in_=ps_t[0:32, 0:128]),
                        after=[ttr])

                def ff_extra(e, ps, c, j):
                    return e.matmul(ps[:], zmT[:, 128 * j:128 * (j + 1)],
                                    lBp_b[:, 512 * c:512 * (c + 1)],
                                    start=False, stop=True)
                if debug:
                    sc.dma("y", lambda e: e.dma_start(out=d_zm, in_=zmT[:]), after=[t_zmT[NB - 1]])
                    sc.dma("y", lambda e: e.dma_start(out=d_lat[1], in_=lAT_b[:]), after=[t_zmT[NB - 1]])
                    sc.dma("y", lambda e: e.dma_start(out=d_zms, in_=zms[:]), after=[t_zmT[NB - 1]])
                stream_bufs["wstg_war"] = [tc1, tc2, tc3]
                ph5_last = stream_matmul_phase(
                    sc, wup_t, lambda ei, j: x2v[:, ei, 128 * j:128 * (j + 1)],
                    x2_sp, x3_sp, wload,
                    extra_mm=ff_extra, extra_deps=lambda j: [t_zmT[j], tc3],
                    resid_ready=ph4b_dma_done)
                if debug:
                    for j in range(NB):
                        sc.dma("y", lambda e, j=j: e.dma_start(
                            out=d_x3[j], in_=x3_sp[j]),
                            after=[ph5_last[(c, j)] for c in range(8)])
                sc.emit("ph5")

        if n_phases >= 6:
            with ExitStack() as p6ctx:
                sbl = mk(p6ctx)
                ln_x = sbl("ln_x6", [128, 4096], F32)
                ln_y = sbl("ln_y6", [128, 4096], F32)
                ln_c = sbl("ln_c6", [128, 16], F32)

                sc = Sched(nc, store, "p6")
                out_toks = []
                prev_st = None
                prev_apply = None
                for j in range(NB):
                    tld = sc.dma("y", lambda e, j=j: e.dma_start(
                        out=ln_x[:], in_=x3_sp[j]),
                        after=[ph5_last[(c, j)] for c in range(8)] + [prev_st, prev_apply])
                    ta1 = ln_block(sc, tld, ln_x, ln_y, ln_c, deps_sq=[prev_st])
                    prev_apply = ta1
                    tst = sc.dma("y", lambda e, j=j: e.dma_start(
                        out=out[j], in_=ln_y[:]), after=[ta1])
                    prev_st = tst
                    out_toks.append(tst)
                sc.add("y", lambda e: None, after=out_toks)
                sc.emit("ph6")

        if n_phases < 6:
            sc = Sched(nc, store, "pF")
            tok = sc.dma("y", lambda e: e.dma_start(out=out[0][:, 0:256], in_=biasT[:, 0:256]))
            sc.add("y", lambda e: None, after=[tok])
            sc.emit("phF")

    return nc, dbg


def rope_tables():
    inv = 1.0 / (10000.0 ** (np.arange(0, D, 2, dtype=np.float64) / D))
    t = np.arange(S, dtype=np.float64)
    fr = np.outer(t, inv)
    emb = np.concatenate([fr, fr], axis=1)
    return np.cos(emb).astype(np.float32), np.sin(emb).astype(np.float32)


def host_inputs(core, x, Wq, Wk, Wv, Wo, p1, b1, p2, b2, Wup, gw, eb, lA, lB):
    b, p = core // 2, core % 2
    blocks = [2 * j + p for j in range(NB)]
    rows = np.concatenate([np.arange(128 * g, 128 * g + 128) for g in blocks])
    xb = np.ascontiguousarray(x[b])
    xT = np.ascontiguousarray(xb.T)
    cos, sin = rope_tables()
    cosT = np.ascontiguousarray(cos.T)
    sgn = np.where(np.arange(128) < 64, -1.0, 1.0).astype(np.float32)
    sinT2 = np.ascontiguousarray(sin.T * sgn[:, None])

    minv = np.zeros((NB, 128, 256), dtype=np.uint8)
    for j in range(NB):
        w = wj(j)
        tcols = np.arange(w - 256, w)
        sglob = 256 * j + 128 * p + np.arange(128)
        minv[j] = (tcols[None, :] > sglob[:, None]).astype(np.uint8)
    m01j0 = (np.arange(256)[None, :] <= (128 * p + np.arange(128))[:, None]).astype(np.float32)

    exp832 = np.zeros((8, 32), dtype=np.float32)
    for h in range(H):
        exp832[h // 4, h] = 1.0

    WqT = Wq.T
    WkT = Wk.T
    WvT = Wv.T
    WoT = Wo.T
    WupT = Wup.T
    xT_own = xT[:, rows]
    loraAT = lA.transpose(2, 0, 1).reshape(4096, 32)
    gwT = gw.T
    return {
        "xt_own": np.ascontiguousarray(
            xT_own.reshape(32, 128, 1024).transpose(1, 0, 2).reshape(128, 32768)),
        "xt_kv": np.ascontiguousarray(
            xT.reshape(32, 128, 2, 1024).transpose(2, 1, 0, 3).reshape(2, 128, 32768)),
        "x_own": np.ascontiguousarray(xb[rows].reshape(NB, 128, 4096)),
        "wq_t": np.ascontiguousarray(
            WqT.reshape(32, 128, 32, 128).transpose(2, 1, 0, 3).reshape(32, 128, 4096)),
        "wk_t": np.ascontiguousarray(
            WkT.reshape(32, 128, 8, 128).transpose(2, 1, 0, 3).reshape(8, 128, 4096)),
        "wv_t": np.ascontiguousarray(
            WvT.reshape(32, 128, 8, 128).transpose(2, 1, 0, 3).reshape(8, 128, 4096)),
        "wo_t": np.ascontiguousarray(
            WoT.reshape(32, 128, 8, 512).transpose(2, 1, 0, 3).reshape(8, 128, 16384)),
        "wup_t": np.ascontiguousarray(
            WupT.reshape(32, 128, 8, 512).transpose(2, 1, 0, 3).reshape(8, 128, 16384)),
        "p1_t": np.ascontiguousarray(p1.transpose(1, 0, 2).reshape(128, 4096)),
        "p2_t": np.ascontiguousarray(p2.transpose(1, 0, 2).reshape(128, 4096)),
        "b1": np.ascontiguousarray(b1[:, 0, :].T),
        "b2": np.ascontiguousarray(b2[:, 0, :].T),
        "cosT_own": np.ascontiguousarray(cosT[:, rows]),
        "sinT2_own": np.ascontiguousarray(sinT2[:, rows]),
        "cosT_kv": cosT,
        "sinT2_kv": sinT2,
        "ident_in": np.eye(128, dtype=np.float32),
        "minv": minv,
        "m01j0": m01j0,
        "exp832": exp832,
        "gw_t": np.ascontiguousarray(
            gwT.reshape(32, 128, 4).transpose(1, 0, 2).reshape(128, 128)),
        "ebias": np.tile(np.asarray(eb, dtype=np.float32)[None, :], (128, 1)),
        "lat_t": np.ascontiguousarray(
            loraAT.reshape(32, 128, 32).transpose(1, 0, 2).reshape(128, 1024)),
        "loraBp": np.ascontiguousarray(lB.transpose(0, 2, 1).reshape(32, 4096)),
    }, rows


def kernel(L0_attn_hidden_states, L0_attn_Wq, L0_attn_Wk, L0_attn_Wv, L0_attn_Wo,
           L0_attn_hash_proj1, L0_attn_hash_bias1, L0_attn_hash_proj2, L0_attn_hash_bias2,
           L0_ffn_W_up, L0_ffn_gate_w, L0_ffn_expert_bias, L0_ffn_lora_A, L0_ffn_lora_B):
    args = (L0_attn_hidden_states, L0_attn_Wq, L0_attn_Wk, L0_attn_Wv, L0_attn_Wo,
            L0_attn_hash_proj1, L0_attn_hash_bias1, L0_attn_hash_proj2, L0_attn_hash_bias2,
            L0_ffn_W_up, L0_ffn_gate_w, L0_ffn_expert_bias, L0_ffn_lora_A, L0_ffn_lora_B)
    nc, _ = build_program()
    in_maps, rowss = [], []
    for c in range(8):
        d, rows = host_inputs(c, *[np.asarray(a) for a in args])
        in_maps.append(d)
        rowss.append(rows)
    res = run_bass_kernel_spmd(nc, in_maps, core_ids=list(range(8)))
    full = np.empty((B, S, E), dtype=np.float32)
    for c in range(8):
        full[c // 2, rowss[c]] = np.asarray(res.results[c]["out"]).reshape(1024, 4096)
    return full


# version 03 (final)
# speedup vs baseline: 1.0400x; 1.0400x over previous
import math
from contextlib import ExitStack

import numpy as np
import ml_dtypes

import concourse.bass as bass
import concourse.mybir as mybir
from concourse.bass_utils import run_bass_kernel_spmd

F32 = mybir.dt.float32
BF16 = mybir.dt.bfloat16
U8 = mybir.dt.uint8
AF = mybir.ActivationFunctionType
ALU = mybir.AluOpType
AX = mybir.AxisListType

B, S, E = 4, 2048, 4096
H, KVH, D = 32, 8, 128
NB = 8
NUM_REMAIN = 410
INV_SQRT_D = 1.0 / math.sqrt(D)
NEG_SENT = -512.0
EPS = 1e-5

DEBUG = False


def wj(j):
    return 256 * (j + 1)


class SemStore:

    def __init__(self, nc, ctx):
        self.sems = {}
        self.counts = {}
        for n in set(Sched.ENG_SEMS.values()) | set(Sched.DMA_SEMS.values()):
            self.sems[n] = ctx.enter_context(nc.semaphore(n))
            self.counts[n] = 0
        self.tokidx = 0
        self.needed = set()


class Sched:
    ENG_SEMS = {"t": "et", "v": "ev", "s": "es", "g": "eg", "y": "ey"}
    DMA_SEMS = {"y": "dy", "g": "dg", "s": "ds"}

    def __init__(self, nc, store, tag):
        self.nc = nc
        self.tag = tag
        self.store = store
        self.items = []
        self.counts = store.counts
        self.sems = store.sems
        self.needed = set()

    def _add(self, eng, fn, after, sem_name, inc):
        waits = []
        for tok in after or []:
            if tok is None:
                continue
            self.store.needed.add(tok[3])
            waits.append(tok)
        self.counts[sem_name] += inc
        tokidx = self.store.tokidx
        self.store.tokidx += 1
        tok = (sem_name, self.counts[sem_name], eng, tokidx)
        self.items.append((eng, fn, waits, sem_name, inc, tokidx, tok))
        return tok

    def add(self, eng, fn, after=None):
        return self._add(eng, fn, after, self.ENG_SEMS[eng], 1)

    def dma(self, eng, fn, after=None):
        return self._add(eng, fn, after, self.DMA_SEMS[eng], 16)

    def emit(self, block_name):
        nc = self.nc
        with nc.Block(block_name) as blk:
            bodies = {"t": blk.tensor, "v": blk.vector, "s": blk.scalar,
                      "g": blk.gpsimd, "y": blk.sync}
            for ek, deco in bodies.items():
                my = [it for it in self.items if it[0] == ek]
                if not my:
                    continue

                def make(my_items, ek=ek):
                    def body(eng):
                        last_seen = {}
                        own = Sched.ENG_SEMS[ek]
                        for (_, fn, waits, sem_name, inc, tokidx, tok) in my_items:
                            ws = {}
                            for (sn, val, peng, ptok) in waits:
                                if ws.get(sn, -1) < val:
                                    ws[sn] = val
                            for sn, val in ws.items():
                                if last_seen.get(sn, -1) >= val:
                                    continue
                                eng.wait_ge(self.sems[sn], val)
                                last_seen[sn] = val
                            inst = fn(eng)
                            if inst is None:
                                inst = eng.nop()
                            inst.then_inc(self.sems[sem_name], inc)
                    return body

                deco(make(my))
        return self


def build_program(n_phases=99, debug=DEBUG, h_lim=H):
    nc = bass.Bass("TRN2", target_bir_lowering=False)
    dt = nc.dram_tensor

    def inp(name, shape, dtype=F32):
        return dt(name, shape, dtype, kind="ExternalInput").ap()

    xt_own = inp("xt_own", [128, 32 * 1024])
    xt_kv = inp("xt_kv", [2, 128, 32 * 1024])
    x_own = inp("x_own", [NB, 128, 4096])
    wq_t = inp("wq_t", [32, 128, 4096])
    wk_t = inp("wk_t", [8, 128, 4096])
    wv_t = inp("wv_t", [8, 128, 4096])
    wo_t = inp("wo_t", [8, 128, 16384])
    wup_t = inp("wup_t", [8, 128, 16384])
    p1_in = inp("p1_t", [128, 4096])
    p2_in = inp("p2_t", [128, 4096])
    b1_in = inp("b1", [128, H])
    b2_in = inp("b2", [128, H])
    cosT_own_in = inp("cosT_own", [128, 1024])
    sinT2_own_in = inp("sinT2_own", [128, 1024])
    cosT_kv_in = inp("cosT_kv", [128, 2048])
    sinT2_kv_in = inp("sinT2_kv", [128, 2048])
    ident_in = inp("ident_in", [128, 128])
    minv_in = inp("minv", [NB, 128, 256], U8)
    m01j0_in = inp("m01j0", [128, 256])
    exp832_in = inp("exp832", [8, 32])
    gwT_in = inp("gw_t", [128, 128])
    ebias_in = inp("ebias", [128, 4])
    loraAT_in = inp("lat_t", [128, 1024])
    loraBp_in = inp("loraBp", [32, 4096])

    out = dt("out", [NB, 128, 4096], F32, kind="ExternalOutput").ap()
    qr_sp = dt("qr_sp", [H, 128, 1024], BF16, kind="Internal").ap()
    qh_sp = dt("qh_sp", [H, 128, 1024], BF16, kind="Internal").ap()
    o_sp = dt("o_sp", [H, 128, 1024], BF16, kind="Internal").ap()
    x2_sp = dt("x2_sp", [NB, 128, 4096], F32, kind="Internal").ap()
    x3_sp = dt("x3_sp", [NB, 128, 4096], F32, kind="Internal").ap()
    nrm_sp = dt("nrm_sp", [32, 1024], F32, kind="Internal").ap()
    k2_sp = dt("k2_sp", [8, 2048], F32, kind="Internal").ap()

    dbg = {}

    def dbgout(name, shape, dtype):
        if debug:
            dbg[name] = dt("dbg_" + name, shape, dtype, kind="ExternalOutput").ap()
            return dbg[name]
        return None

    d_qr = dbgout("qr", [H, 128, 1024], BF16)
    d_qh = dbgout("qh", [H, 128, 1024], BF16)
    d_nrm = dbgout("nrm", [32, 1024], F32)
    d_krT = dbgout("krT", [128, 8 * 2048], BF16)
    d_v = dbgout("v", [128, 16 * 1024], BF16)
    d_k2 = dbgout("k2", [32, 1], F32)
    d_biasT = dbgout("biasT", [128, NB * 32], F32)
    d_k2r = dbgout("k2r", [8, 2048], F32)
    d_k2m = dbgout("k2m", [2, 8, 1], F32)
    d_kh = dbgout("kh", [H, 128, 2048], BF16)
    d_o = dbgout("o", [H, 128, 1024], BF16)
    d_x2 = dbgout("x2", [NB, 128, 4096], F32)
    d_x3 = dbgout("x3", [NB, 128, 4096], F32)
    d_zm = dbgout("zm", [32, NB * 128], BF16)
    d_zms = dbgout("zms", [128, NB * 32], BF16)
    d_oh = dbgout("oh", [NB, 128, 8], F32)
    d_lg = dbgout("lg", [NB, 128, 8], F32)
    d_ps = dbgout("ps", [128, 64], F32)
    d_lat = dbgout("lat", [2, 128, 1024], BF16)
    d_tau = dbgout("tau", [NB, 128, H], F32)
    d_rr = dbgout("rr", [NB, 128, H], F32)
    d_praw = dbgout("praw", [128, 2048], BF16)
    d_pp = dbgout("pp", [128, 2048], BF16)
    d_z = dbgout("z", [128, 8], F32)
    d_m01 = dbgout("m01", [128, 2048], BF16)
    d_scb = dbgout("scb", [128, 2048], F32)

    octx = ExitStack()
    with octx:
        def mk(ctx_):
            def sbf(name, shape, dtype):
                return ctx_.enter_context(nc.sbuf_tensor(name, shape, dtype))
            return sbf
        sb = mk(octx)
        psum = lambda name, shape, dtype=F32: octx.enter_context(nc.psum_tensor(name, shape, dtype))
        store = SemStore(nc, octx)

        ident_b = sb("ident_b", [128, 128], BF16)
        p1b = sb("p1b", [128, H * 128], BF16)
        p2b = sb("p2b", [128, H * 128], BF16)
        b1_s = sb("b1_s", [128, H], F32)
        b2_s = sb("b2_s", [128, H], F32)
        m01j0 = sb("m01j0_s", [128, 256], BF16)
        minv_s = sb("minv_s", [128, NB * 256], U8)
        neg512 = sb("neg512", [128, 256], BF16)
        zeros_b = sb("zeros_b", [128, 2048], BF16)
        ones_b = sb("ones_b", [128, 1], BF16)
        exp832_b = sb("exp832_b", [8, 32], BF16)
        biasT = sb("biasT", [128, NB * 32], F32)

        ps_a = psum("ps_a", [128, 512], F32)
        ps_b = psum("ps_b", [128, 512], F32)
        ps_c = psum("ps_c", [128, 512], F32)
        ps_d = psum("ps_d", [128, 512], F32)
        ps_t = psum("ps_t", [128, 1024], BF16)
        ps_n = psum("ps_n", [1, 512], F32)
        ps_s = psum("ps_s", [128, 128], F32)
        ps4 = [ps_a, ps_b, ps_c, ps_d]

        sc = Sched(nc, store, "p0")
        consts_done = []
        with ExitStack() as p0ctx:
            stg0 = mk(p0ctx)("stg0", [128, 4096], F32)
            for fn in [
                lambda e: e.dma_start(out=b1_s[:], in_=b1_in),
                lambda e: e.dma_start(out=b2_s[:], in_=b2_in),
                lambda e: e.dma_start(out=minv_s[:].rearrange("p (n c) -> p n c", n=NB),
                                      in_=minv_in.rearrange("n p c -> p n c")),
            ]:
                consts_done.append(sc.dma("y", fn))

            def load_cast(dram_ap, dest_ap, after=None, stg_ap=None):
                stg = stg_ap if stg_ap is not None else stg0[:]
                tl = sc.dma("y", lambda e: e.dma_start(out=stg, in_=dram_ap),
                            after=after or [])
                return sc.add("v", lambda e: e.tensor_copy(out=dest_ap, in_=stg),
                              after=[tl])

            t1 = load_cast(ident_in, ident_b[:], stg_ap=stg0[:, 0:128])
            t2 = load_cast(p1_in, p1b[:], after=[t1])
            t3 = load_cast(p2_in, p2b[:], after=[t2])
            t4 = load_cast(m01j0_in, m01j0[:], after=[t3], stg_ap=stg0[:, 0:256])
            t5 = load_cast(exp832_in, exp832_b[:], after=[t4], stg_ap=stg0[0:8, 0:32])
            consts_done += [t1, t2, t3, t4, t5]

            def memsets(e):
                e.memset(neg512[:], NEG_SENT)
                e.memset(zeros_b[:], 0.0)
                return e.memset(ones_b[:], 1.0)
            consts_done.append(sc.add("v", memsets))
            sc.emit("ph0")

        if n_phases >= 1:
            with ExitStack() as p1ctx:
                sbl = mk(p1ctx)
                xb = sbl("xb", [128, 32 * 1024], BF16)
                wstg = sbl("wstg1", [128, 4096], F32)
                wload = sbl("wload", [128, 2 * 4096], BF16)
                cosq = sbl("cosq", [128, 1024], F32)
                sinq = sbl("sinq", [128, 1024], F32)
                hash_f = sbl("hash_f", [128, 1024], F32)
                hash_g = sbl("hash_g", [128, 1024], F32)
                hash_tb = sbl("hash_tb", [128, 1024], BF16)
                qr_bf = sbl("qr_bf", [128, 1024], BF16)
                qh_bf = sbl("qh_bf", [128, 1024], BF16)
                sq_bf = sbl("sq_bf", [128, 1024], BF16)
                tmp1r = sbl("tmp1r", [1, 1024], F32)

                sc = Sched(nc, store, "p1")
                t_cos = sc.dma("y", lambda e: e.dma_start(out=cosq[:], in_=cosT_own_in))
                t_sin = sc.dma("y", lambda e: e.dma_start(out=sinq[:], in_=sinT2_own_in))
                xv = xb[:].rearrange("p (n c) -> p n c", n=32)
                t_x = []
                prevc = None
                for i in range(8):
                    tl = sc.dma("y", lambda e, i=i: e.dma_start(
                        out=wstg[:], in_=xt_own[:, 4096 * i:4096 * (i + 1)]),
                        after=[prevc])
                    prevc = sc.add("v", lambda e, i=i: e.tensor_copy(
                        out=xb[:, 4096 * i:4096 * (i + 1)], in_=wstg[:]), after=[tl])
                    t_x.append(prevc)
                wv = wload[:].rearrange("p (n c) -> p n c", n=64)

                t_wq, t_qp, t_rope, t_mm1, t_silu, t_t1, t_mm2, t_z2b, t_th, t_sgn = \
                    ([None] * H for _ in range(10))
                t_nma, t_nca, t_nmb, t_ncb, t_spq, t_sph, t_spn = ([None] * H for _ in range(7))

                t_wc = [None] * H
                for eo in range(h_lim):
                    half = 32 * (eo % 2)
                    t_wq[eo] = sc.dma("y", lambda e, eo=eo: e.dma_start(
                        out=wstg[:], in_=wq_t[eo]),
                        after=[t_wc[eo - 1] if eo else None] + t_x)
                    t_wc[eo] = sc.add("v", lambda e, half=half: e.tensor_copy(
                        out=wload[:, 4096 * (half // 32):4096 * (half // 32) + 4096],
                        in_=wstg[:]),
                        after=[t_wq[eo]] + ([t_qp[eo - 2]] if eo >= 2 else []))

                    def qproj(e, half=half):
                        for hx, ps in ((0, ps_a), (1, ps_b)):
                            for ei in range(32):
                                inst = e.matmul(
                                    ps[:], wv[:, half + ei, :],
                                    xb[:, 1024 * ei + 512 * hx:1024 * ei + 512 * (hx + 1)],
                                    start=(ei == 0), stop=(ei == 31))
                        return inst
                    t_qp[eo] = sc.add("t", qproj,
                                      after=[t_wc[eo], t_rope[eo - 1] if eo else None])

                    def rope(e):
                        e.tensor_mul(out=hash_f[:, 0:512], in0=ps_a[:], in1=cosq[:, 0:512])
                        e.tensor_mul(out=hash_f[:, 512:1024], in0=ps_b[:], in1=cosq[:, 512:1024])
                        for base, src in ((0, ps_a), (512, ps_b)):
                            e.tensor_mul(out=hash_g[0:64, base:base + 512],
                                         in0=src[64:128, :], in1=sinq[0:64, base:base + 512])
                            e.tensor_mul(out=hash_g[64:128, base:base + 512],
                                         in0=src[0:64, :], in1=sinq[64:128, base:base + 512])
                        return e.tensor_add(out=qr_bf[:], in0=hash_f[:], in1=hash_g[:])
                    t_rope[eo] = sc.add("v", rope,
                                        after=[t_qp[eo], t_cos, t_sin,
                                               t_spq[eo - 1] if eo else None,
                                               t_sgn[eo - 1] if eo else None,
                                               t_mm1[eo - 1] if eo else None])

                    def mm1(e, eo=eo):
                        e.matmul(ps_c[:], p1b[:, 128 * eo:128 * (eo + 1)], qr_bf[:, 0:512],
                                 start=True, stop=True)
                        return e.matmul(ps_d[:], p1b[:, 128 * eo:128 * (eo + 1)],
                                        qr_bf[:, 512:1024], start=True, stop=True)
                    t_mm1[eo] = sc.add("t", mm1,
                                       after=[t_rope[eo], t_z2b[eo - 1] if eo else None]
                                       + consts_done)

                    def silu(e, eo=eo):
                        e.activation(out=hash_g[:, 0:512], in_=ps_c[:], func=AF.Silu,
                                     bias=b1_s[:, eo:eo + 1], scale=1.0)
                        return e.activation(out=hash_g[:, 512:1024], in_=ps_d[:], func=AF.Silu,
                                            bias=b1_s[:, eo:eo + 1], scale=1.0)
                    t_silu[eo] = sc.add("s", silu, after=[t_mm1[eo], t_th[eo - 1] if eo else None])

                    def t1(e):
                        e.tensor_add(out=hash_f[:], in0=hash_g[:], in1=qr_bf[:])
                        return e.tensor_copy(out=hash_tb[:], in_=hash_f[:])
                    t_t1[eo] = sc.add("v", t1, after=[t_silu[eo]])

                    def mm2(e, eo=eo):
                        e.matmul(ps_c[:], p2b[:, 128 * eo:128 * (eo + 1)], hash_tb[:, 0:512],
                                 start=True, stop=True)
                        return e.matmul(ps_d[:], p2b[:, 128 * eo:128 * (eo + 1)],
                                        hash_tb[:, 512:1024], start=True, stop=True)
                    t_mm2[eo] = sc.add("t", mm2, after=[t_t1[eo]])

                    def z2b(e, eo=eo):
                        e.activation(out=hash_g[:, 0:512], in_=ps_c[:], func=AF.Identity,
                                     bias=b2_s[:, eo:eo + 1], scale=1.0)
                        return e.activation(out=hash_g[:, 512:1024], in_=ps_d[:], func=AF.Identity,
                                            bias=b2_s[:, eo:eo + 1], scale=1.0)
                    t_z2b[eo] = sc.add("s", z2b, after=[t_mm2[eo]])

                    t_th[eo] = sc.add("v", lambda e: e.tensor_add(
                        out=hash_f[:], in0=hash_g[:], in1=hash_f[:]),
                        after=[t_z2b[eo]])

                    def sgn(e):
                        e.activation(out=qh_bf[:], in_=hash_f[:], func=AF.Sign)
                        return e.activation(out=sq_bf[:], in_=qr_bf[:], func=AF.Square)
                    t_sgn[eo] = sc.add("s", sgn,
                                       after=[t_th[eo], t_sph[eo - 1] if eo else None,
                                              t_nmb[eo - 1] if eo else None])

                    t_nma[eo] = sc.add("t", lambda e: e.matmul(
                        ps_n[:], ones_b[:], sq_bf[:, 0:512], start=True, stop=True),
                        after=[t_sgn[eo], t_nca[eo - 1] if eo else None])
                    t_nca[eo] = sc.add("v", lambda e: e.tensor_copy(
                        out=tmp1r[:, 0:512], in_=ps_n[:]), after=[t_nma[eo]])
                    t_nmb[eo] = sc.add("t", lambda e: e.matmul(
                        ps_n[:], ones_b[:], sq_bf[:, 512:1024], start=True, stop=True),
                        after=[t_nca[eo]])
                    t_ncb[eo] = sc.add("v", lambda e: e.tensor_copy(
                        out=tmp1r[:, 512:1024], in_=ps_n[:]),
                        after=[t_nmb[eo], t_spn[eo - 1] if eo else None])

                    t_spq[eo] = sc.dma("y", lambda e, eo=eo: e.dma_start(
                        out=qr_sp[eo], in_=qr_bf[:]), after=[t_rope[eo]])
                    t_sph[eo] = sc.dma("y", lambda e, eo=eo: e.dma_start(
                        out=qh_sp[eo], in_=qh_bf[:]), after=[t_sgn[eo]])
                    t_spn[eo] = sc.dma("y", lambda e, eo=eo: e.dma_start(
                        out=nrm_sp[eo:eo + 1, :], in_=tmp1r[:]), after=[t_ncb[eo]])

                ph1_dma_done = t_spn[h_lim - 1]
                if debug:
                    for eo in range(h_lim):
                        sc.dma("y", lambda e, eo=eo: e.dma_start(
                            out=d_qr[eo], in_=qr_sp[eo]), after=[t_spq[eo]])
                        sc.dma("y", lambda e, eo=eo: e.dma_start(
                            out=d_qh[eo], in_=qh_sp[eo]), after=[t_sph[eo]])
                    sc.dma("y", lambda e: e.dma_start(out=d_nrm, in_=nrm_sp),
                           after=[t_spn[h_lim - 1]])
                sc.emit("ph1")

        mid = ExitStack()
        octx.enter_context(mid)
        sbm = mk(mid)
        krT = sbm("krT", [128, 8 * 2048], BF16)
        vtok = sbm("vtok", [128, 16 * 1024], BF16)

        if n_phases >= 2:
            with ExitStack() as p2ctx:
                sbl = mk(p2ctx)
                xq = sbl("xq", [128, 32 * 1024], BF16)
                wstg = sbl("wstg2", [128, 4096], F32)
                wload = sbl("wload2", [128, 2 * 4096], BF16)
                cosq = sbl("cosq2", [128, 1024], F32)
                sinq = sbl("sinq2", [128, 1024], F32)
                kr_f = sbl("kr_f", [128, 1024], F32)
                kr_g = sbl("kr_g", [128, 1024], F32)
                sq_bf = sbl("sq_bf2", [128, 1024], BF16)
                vc_bf = sbl("vc_bf", [128, 1024], BF16)
                tmp1r = sbl("tmp1r2", [1, 512], F32)

                sc = Sched(nc, store, "p2")
                xv = xq[:].rearrange("p (n c) -> p n c", n=32)
                wv = wload[:].rearrange("p (n c) -> p n c", n=64)

                t_wt = {}
                t_pj = {}
                t_done = {}
                t_k2w = {}
                pj_hist = []
                done_hist = []
                prev_ropevac = None
                prev_sq = None
                prev_nm = [None, None]
                prev_vtr = None
                prev_vevac = None
                t_cos = t_sin = None
                t_wcast = {}
                for half in range(2):
                    t_x = []
                    xdep = [t for k, t in t_pj.items()]
                    rope_dep = [t for k, t in t_done.items() if k[1] < 8]
                    t_cos = sc.dma("y", lambda e, half=half: e.dma_start(
                        out=cosq[:], in_=cosT_kv_in[:, 1024 * half:1024 * (half + 1)]),
                        after=rope_dep)
                    t_sin = sc.dma("y", lambda e, half=half: e.dma_start(
                        out=sinq[:], in_=sinT2_kv_in[:, 1024 * half:1024 * (half + 1)]),
                        after=rope_dep)
                    prevc = t_wcast.get((1 - half, 15))
                    for i in range(8):
                        tl = sc.dma("y", lambda e, i=i, half=half: e.dma_start(
                            out=wstg[:], in_=xt_kv[half, :, 4096 * i:4096 * (i + 1)]),
                            after=[prevc] + xdep)
                        prevc = sc.add("v", lambda e, i=i: e.tensor_copy(
                            out=xq[:, 4096 * i:4096 * (i + 1)], in_=wstg[:]), after=[tl])
                        t_x.append(prevc)
                    for eo in range(16):
                        isk = eo < 8
                        W, col = (wk_t, eo) if isk else (wv_t, eo - 8)
                        wslot = 32 * (eo % 2)
                        t_wt[(half, eo)] = sc.dma(
                            "y", lambda e, W=W, col=col: e.dma_start(
                                out=wstg[:], in_=W[col]),
                            after=[t_wcast.get((half, eo - 1)) if eo else
                                   (t_wcast.get((1 - half, 15)) if half else None)] + t_x)
                        t_wcast[(half, eo)] = sc.add(
                            "v", lambda e, wslot=wslot: e.tensor_copy(
                                out=wload[:, 128 * wslot:128 * wslot + 4096], in_=wstg[:]),
                            after=[t_wt[(half, eo)],
                                   pj_hist[-2] if len(pj_hist) >= 2 else None])

                        def proj(e, wslot=wslot):
                            for hx, ps in ((0, ps_a), (1, ps_b)):
                                for ei in range(32):
                                    inst = e.matmul(
                                        ps[:], wv[:, wslot + ei, :],
                                        xq[:, 1024 * ei + 512 * hx:1024 * ei + 512 * (hx + 1)],
                                        start=(ei == 0), stop=(ei == 31))
                            return inst
                        t_pj[(half, eo)] = sc.add("t", proj,
                                                  after=[t_wcast[(half, eo)],
                                                         done_hist[-1] if done_hist else None]
                                                  + t_x)
                        pj_hist.append(t_pj[(half, eo)])

                        if isk:
                            def ropek_c(e, eo=eo, half=half):
                                cs = cosq[:]
                                sn = sinq[:]
                                e.tensor_mul(out=kr_f[:, 0:512], in0=ps_a[:], in1=cs[:, 0:512])
                                e.tensor_mul(out=kr_f[:, 512:1024], in0=ps_b[:], in1=cs[:, 512:1024])
                                for base, src in ((0, ps_a), (512, ps_b)):
                                    e.tensor_mul(out=kr_g[0:64, base:base + 512],
                                                 in0=src[64:128, :], in1=sn[0:64, base:base + 512])
                                    e.tensor_mul(out=kr_g[64:128, base:base + 512],
                                                 in0=src[0:64, :], in1=sn[64:128, base:base + 512])
                                e.tensor_add(out=kr_f[:], in0=kr_f[:], in1=kr_g[:])
                                return e.tensor_copy(
                                    out=krT[:, 2048 * eo + 1024 * half:2048 * eo + 1024 * (half + 1)],
                                    in_=kr_f[:])
                            tv = sc.add("v", ropek_c,
                                        after=[t_pj[(half, eo)], t_cos, t_sin, prev_sq])
                            t_done[(half, eo)] = tv
                            done_hist.append(tv)
                            ts_ = sc.add("s", lambda e: e.activation(
                                out=sq_bf[:], in_=kr_f[:], func=AF.Square),
                                after=[tv, prev_nm[1]])
                            prev_sq = ts_
                            na = sc.add("t", lambda e: e.matmul(
                                ps_n[:], ones_b[:], sq_bf[:, 0:512], start=True, stop=True),
                                after=[ts_, prev_nm[0]])
                            ca = sc.add("v", lambda e: e.tensor_copy(
                                out=tmp1r[:], in_=ps_n[:]), after=[na])
                            da = sc.dma("y", lambda e, eo=eo, half=half: e.dma_start(
                                out=k2_sp[eo:eo + 1, 1024 * half:1024 * half + 512], in_=tmp1r[:]),
                                after=[ca])
                            nb = sc.add("t", lambda e: e.matmul(
                                ps_n[:], ones_b[:], sq_bf[:, 512:1024], start=True, stop=True),
                                after=[ca])
                            prev_nm[0] = nb
                            cb = sc.add("v", lambda e: e.tensor_copy(
                                out=tmp1r[:], in_=ps_n[:]),
                                after=[nb, da])
                            prev_nm[1] = cb
                            t_k2w[(half, eo)] = sc.dma("y", lambda e, eo=eo, half=half: e.dma_start(
                                out=k2_sp[eo:eo + 1, 1024 * half + 512:1024 * (half + 1)], in_=tmp1r[:]),
                                after=[cb])
                        else:
                            def vevac(e):
                                e.tensor_copy(out=vc_bf[:, 0:512], in_=ps_a[:])
                                return e.tensor_copy(out=vc_bf[:, 512:1024], in_=ps_b[:])
                            tv = sc.add("v", vevac, after=[t_pj[(half, eo)], prev_vtr])
                            t_done[(half, eo)] = tv
                            done_hist.append(tv)

                            def vtr(e):
                                for k in range(8):
                                    inst = e.transpose(
                                        ps_t[:, 128 * k:128 * (k + 1)],
                                        vc_bf[:, 128 * k:128 * (k + 1)], ident_b[:])
                                return inst
                            ttr = sc.add("t", vtr, after=[tv, prev_vevac] + consts_done)
                            prev_vtr = ttr

                            def vout(e, eo=eo, half=half):
                                vview = vtok[:].rearrange("p (tb c) -> p tb c", tb=16)
                                pview = ps_t[:].rearrange("p (tb c) -> p tb c", tb=8)
                                return e.tensor_copy(
                                    out=vview[:, 8 * half:8 * (half + 1),
                                              128 * (eo - 8):128 * (eo - 7)],
                                    in_=pview[:])
                            tvo = sc.add("v", vout, after=[ttr])
                            prev_vevac = tvo

                last_k2 = [t_k2w[(1, eo)] for eo in range(8)]
                if debug:
                    sc.dma("y", lambda e: e.dma_start(out=d_krT, in_=krT[:]),
                           after=[t_done[(h, eo)] for h in range(2) for eo in range(8)])
                    sc.dma("y", lambda e: e.dma_start(out=d_v, in_=vtok[:]),
                           after=[prev_vevac])
                sc.emit("ph2")

            with ExitStack() as p2bctx:
                sbl = mk(p2bctx)
                k2r = sbl("k2r", [8, 2048], F32)
                k2m = sbl("k2m", [8, 1], F32)
                k2mb = sbl("k2mb", [8, 1], BF16)
                k2h = sbl("k2h", [32, 1], F32)
                nrm2r = sbl("nrm2r", [32, 1024], F32)
                bm_b = sbl("bm_b", [32, 1024], BF16)

                sc = Sched(nc, store, "p2b")
                tk2 = sc.dma("y", lambda e: e.dma_start(out=k2r[:], in_=k2_sp),
                             after=last_k2)
                t_nrm = sc.dma("y", lambda e: e.dma_start(out=nrm2r[:], in_=nrm_sp),
                               after=[ph1_dma_done])
                t_r1 = sc.add("v", lambda e: e.reduce_max(k2m[:], k2r[:], axis=AX.X),
                              after=[tk2])
                t_r2 = sc.add("v", lambda e: e.tensor_copy(out=k2mb[:], in_=k2m[:]),
                              after=[t_r1])
                t_r3 = sc.add("t", lambda e: e.matmul(
                    ps_s[0:32, 0:1], exp832_b[:], k2mb[:], start=True, stop=True),
                    after=[t_r2] + consts_done)
                t_r4 = sc.add("v", lambda e: e.tensor_copy(out=k2h[:], in_=ps_s[0:32, 0:1]),
                              after=[t_r3])
                t_r6 = sc.add("v", lambda e: e.tensor_scalar(
                    out=bm_b[:], in0=nrm2r[:], scalar1=k2h[:], scalar2=-0.5 * INV_SQRT_D,
                    op0=ALU.add, op1=ALU.mult), after=[t_r4, t_nrm])
                t_bj = []
                for j in range(NB):
                    ta = sc.add("t", lambda e, j=j: e.transpose(
                        ps_t[:, 0:32], bm_b[:, 128 * j:128 * (j + 1)], ident_b[0:32, 0:32]),
                        after=[t_r6] + ([t_bj[-1]] if t_bj else [t_r4]))
                    tb = sc.add("v", lambda e, j=j: e.tensor_copy(
                        out=biasT[:, 32 * j:32 * (j + 1)], in_=ps_t[:, 0:32]), after=[ta])
                    t_bj.append(tb)
                if debug:
                    sc.dma("y", lambda e: e.dma_start(out=d_k2, in_=k2h[:]), after=[t_r4])
                    sc.dma("y", lambda e: e.dma_start(out=d_k2r, in_=k2r[:]), after=[t_r1])
                    sc.dma("y", lambda e: e.dma_start(out=d_k2m[0], in_=k2m[:]), after=[t_r2])
                    sc.dma("y", lambda e: e.dma_start(out=d_biasT, in_=biasT[:]),
                           after=t_bj)
                sc.emit("ph2b")

        if n_phases >= 3:
            with ExitStack() as p3ctx:
                sbl = mk(p3ctx)
                khT = sbl("khT", [128, 4 * 2048], BF16)
                simt = sbl("simt", [128, 4 * 2048], BF16)
                qrg = sbl("qrg", [128, 2 * 4 * 128], BF16)
                qhg = sbl("qhg", [128, 2 * 4 * 128], BF16)
                cum_f = sbl("cum_f", [128, 2048], F32)
                eqs_b = sbl("eqs_b", [128, 2048], BF16)
                m01_b = sbl("m01_b", [128, 2048], BF16)
                praw_b = sbl("praw_b", [128, 2048], BF16)
                ptr_b = sbl("ptr_b", [128, 2048], BF16)
                o_hg = sbl("o_hg", [128, 4 * 128], BF16)
                scr_bf = sbl("scr_bf3", [128, 2048], BF16)
                st_f = sbl("st_f", [128, 64], F32)
                st_u = sbl("st_u", [128, 8], U8)
                zcol = sbl("zcol", [128, 8], F32)

                sc = Sched(nc, store, "p3")

                def chunks_of(w):
                    out = []
                    c0 = 0
                    while c0 < w:
                        cw = min(512, w - c0)
                        out.append((c0, cw))
                        c0 += cw
                    return out

                prev_kh = {}
                prev_ps4 = None
                prev_cum = None
                prev_eqs = None
                prev_m_items = []
                prev_oev = [None] * 4
                prev_osp = None
                prev_pst = None
                prev_pss = None
                prev_ql = [None, None]
                prev_simuse = []
                m4_hist = []
                av_hist = []
                last_khT_reader = [None] * 4

                for hg in range(8):
                    g = hg
                    kh_done = []
                    for i in range(4):
                        h = 4 * hg + i

                        def kh1(e, h=h, g=g):
                            for c in range(4):
                                inst = e.matmul(ps4[c][:], p1b[:, 128 * h:128 * (h + 1)],
                                                krT[:, 2048 * g + 512 * c:2048 * g + 512 * (c + 1)],
                                                start=True, stop=True)
                            return inst
                        tk1 = sc.add("t", kh1, after=[prev_ps4, last_khT_reader[i]] + kh_done)

                        def kh2(e, h=h):
                            for c in range(4):
                                inst = e.activation(out=cum_f[:, 512 * c:512 * (c + 1)],
                                                    in_=ps4[c][:], func=AF.Silu,
                                                    bias=b1_s[:, h:h + 1], scale=1.0)
                            return inst
                        tk2 = sc.add("s", kh2, after=[tk1, prev_cum])

                        def kh3(e, g=g):
                            e.tensor_add(out=cum_f[:], in0=cum_f[:],
                                         in1=krT[:, 2048 * g:2048 * (g + 1)])
                            return e.tensor_copy(out=eqs_b[:], in_=cum_f[:])
                        tk3 = sc.add("v", kh3, after=[tk2, prev_eqs])

                        def kh4(e, h=h):
                            for c in range(4):
                                inst = e.matmul(ps4[c][:], p2b[:, 128 * h:128 * (h + 1)],
                                                eqs_b[:, 512 * c:512 * (c + 1)],
                                                start=True, stop=True)
                            return inst
                        tk4 = sc.add("t", kh4, after=[tk3, tk2])

                        def kh5(e, h=h):
                            e.tensor_scalar(out=cum_f[:], in0=cum_f[:], scalar1=b2_s[:, h:h + 1],
                                            scalar2=None, op0=ALU.add)
                            for c in range(4):
                                inst = e.tensor_add(out=cum_f[:, 512 * c:512 * (c + 1)],
                                                    in0=cum_f[:, 512 * c:512 * (c + 1)],
                                                    in1=ps4[c][:])
                            return inst
                        tk5 = sc.add("v", kh5, after=[tk4])

                        tk6 = sc.add("s", lambda e, i=i: e.activation(
                            out=khT[:, 2048 * i:2048 * (i + 1)], in_=cum_f[:], func=AF.Sign),
                            after=[tk5, last_khT_reader[i]])
                        kh_done.append(tk6)
                        prev_ps4 = tk5
                        prev_cum = tk6
                        prev_eqs = tk4
                        prev_kh[i] = tk6

                    if debug:
                        for i in range(4):
                            sc.dma("y", lambda e, i=i, hg=hg: e.dma_start(
                                out=d_kh[4 * hg + i], in_=khT[:, 2048 * i:2048 * (i + 1)]),
                                after=[kh_done[i]])

                    prev_simuse = [None] * 4
                    for j in range(NB):
                        w = wj(j)
                        par = j % 2
                        qsl = 512 * par
                        ch = chunks_of(w)
                        srch = j >= 1

                        tql = sc.dma("y", lambda e, hg=hg, j=j, qsl=qsl: e.dma_start(
                            out=qrg[:, qsl:qsl + 512].rearrange("p (n c) -> p n c", n=4),
                            in_=qr_sp[4 * hg:4 * hg + 4, :, 128 * j:128 * (j + 1)]
                            .rearrange("n p c -> p n c")), after=[prev_ql[par], ph1_dma_done])
                        tqh = sc.dma("y", lambda e, hg=hg, j=j, qsl=qsl: e.dma_start(
                            out=qhg[:, qsl:qsl + 512].rearrange("p (n c) -> p n c", n=4),
                            in_=qh_sp[4 * hg:4 * hg + 4, :, 128 * j:128 * (j + 1)]
                            .rearrange("n p c -> p n c")), after=[prev_ql[par], ph1_dma_done])

                        sev = [None] * 4
                        if srch:
                            for i in range(4):
                                def simmm(e, i=i, qsl=qsl, ch=ch):
                                    for c, (c0, cw) in enumerate(ch):
                                        inst = e.matmul(
                                            ps4[c % 4][:, 0:cw],
                                            qhg[:, qsl + 128 * i:qsl + 128 * (i + 1)],
                                            khT[:, 2048 * i + c0:2048 * i + c0 + cw],
                                            start=True, stop=True)
                                    return inst
                                tsm = sc.add("t", simmm,
                                             after=[tqh, prev_ps4, kh_done[i]])
                                last_khT_reader[i] = tsm

                                def simev(e, i=i, j=j, ch=ch, w=w):
                                    for (c0, cw) in ch:
                                        e.tensor_copy(
                                            out=simt[:, 2048 * i + c0:2048 * i + c0 + cw],
                                            in_=ps4[(c0 // 512) % 4][:, 0:cw])
                                    return e.copy_predicated(
                                        out=simt[:, 2048 * i + w - 256:2048 * i + w],
                                        mask=minv_s[:, 256 * j:256 * (j + 1)],
                                        data=neg512[:])
                                sev[i] = sc.add("v", simev, after=[tsm] + prev_simuse)
                                prev_ps4 = sev[i]

                        m_items = []
                        if srch:
                            t0 = sc.add("v", lambda e: e.memset(st_f[:, 0:4], -128.0),
                                        after=prev_m_items + sev)
                            prev_upd2 = None
                            prev_cs = t0
                            prev_cv = t0
                            last_upd3 = t0
                            for it, k in enumerate(range(8, -1, -1)):
                                step = float(2 ** k)

                                def sa(e, step=step):
                                    e.tensor_scalar(out=st_f[:, 4:8], in0=st_f[:, 0:4],
                                                    scalar1=step, scalar2=None, op0=ALU.add)
                                    e.tensor_scalar(out=st_f[:, 8:12], in0=st_f[:, 0:4],
                                                    scalar1=step - 0.5, scalar2=None, op0=ALU.add)
                                    return e.tensor_scalar(
                                        out=st_f[:, 12:16], in0=st_f[:, 0:4],
                                        scalar1=-1.0, scalar2=-(step - 0.5),
                                        op0=ALU.mult, op1=ALU.add)
                                tsa = sc.add("v", sa, after=[last_upd3])

                                def cs(e, w=w):
                                    for i in range(2):
                                        inst = e.activation(
                                            out=scr_bf[:, 0:w], in_=simt[:, 2048 * i:2048 * i + w],
                                            func=AF.Sign, bias=st_f[:, 12 + i:13 + i], scale=1.0,
                                            accum_out=st_f[:, 16 + i:17 + i])
                                    return inst
                                tcs = sc.add("s", cs, after=[tsa, prev_upd2, prev_cs])
                                prev_cs = tcs

                                def cv(e, w=w):
                                    for i in range(2, 4):
                                        inst = e.tensor_scalar(
                                            out=praw_b[:, 0:w], in0=simt[:, 2048 * i:2048 * i + w],
                                            scalar1=st_f[:, 8 + i:9 + i], scalar2=None,
                                            op0=ALU.is_ge, op1=ALU.add,
                                            accum_out=st_f[:, 16 + i:17 + i])
                                    return inst
                                tcv = sc.add("v", cv, after=[tsa, prev_upd2, prev_cv])
                                prev_cv = tcv

                                tu1 = sc.add("v", lambda e, w=w: e.tensor_scalar(
                                    out=st_f[:, 16:18], in0=st_f[:, 16:18],
                                    scalar1=0.5, scalar2=w * 0.5,
                                    op0=ALU.mult, op1=ALU.add), after=[tcs])
                                tu2 = sc.add("v", lambda e: e.tensor_scalar(
                                    out=st_u[:, 0:4], in0=st_f[:, 16:20],
                                    scalar1=float(NUM_REMAIN), scalar2=None,
                                    op0=ALU.is_ge), after=[tu1, tcv])
                                prev_upd2 = tu2
                                tu3 = sc.add("v", lambda e: e.copy_predicated(
                                    out=st_f[:, 0:4], mask=st_u[:, 0:4],
                                    data=st_f[:, 4:8]), after=[tu2])
                                last_upd3 = tu3

                            def fin1(e):
                                e.tensor_scalar(out=st_f[:, 8:12], in0=st_f[:, 0:4],
                                                scalar1=0.5, scalar2=None, op0=ALU.add)
                                return e.tensor_scalar(out=st_f[:, 12:16], in0=st_f[:, 0:4],
                                                       scalar1=-1.0, scalar2=-0.5,
                                                       op0=ALU.mult, op1=ALU.add)
                            tf1 = sc.add("v", fin1, after=[last_upd3])
                            tf2 = sc.add("s", lambda e, w=w: [
                                e.activation(out=scr_bf[:, 0:w], in_=simt[:, 2048 * i:2048 * i + w],
                                             func=AF.Sign, bias=st_f[:, 12 + i:13 + i], scale=1.0,
                                             accum_out=st_f[:, 16 + i:17 + i])
                                for i in range(2)][-1], after=[tf1, prev_cs])
                            tf3 = sc.add("v", lambda e, w=w: [
                                e.tensor_scalar(out=praw_b[:, 0:w],
                                                in0=simt[:, 2048 * i:2048 * i + w],
                                                scalar1=st_f[:, 8 + i:9 + i], scalar2=None,
                                                op0=ALU.is_ge, op1=ALU.add,
                                                accum_out=st_f[:, 16 + i:17 + i])
                                for i in range(2, 4)][-1], after=[tf1, prev_cv])
                            tf4 = sc.add("v", lambda e, w=w: e.tensor_scalar(
                                out=st_f[:, 16:18], in0=st_f[:, 16:18],
                                scalar1=0.5, scalar2=w * 0.5,
                                op0=ALU.mult, op1=ALU.add), after=[tf2])
                            tf5 = sc.add("v", lambda e: e.tensor_scalar(
                                out=st_f[:, 20:24], in0=st_f[:, 16:20],
                                scalar1=-1.0, scalar2=float(NUM_REMAIN),
                                op0=ALU.mult, op1=ALU.add), after=[tf4, tf3])
                            tf6 = sc.add("v", lambda e: e.tensor_scalar(
                                out=st_f[:, 24:28], in0=st_f[:, 20:24],
                                scalar1=float(2 ** -12), scalar2=None,
                                op0=ALU.mult), after=[tf5])
                            tf7 = sc.add("v", lambda e: e.tensor_sub(
                                out=st_f[:, 28:32], in0=st_f[:, 0:4], in1=st_f[:, 24:28]),
                                after=[tf6])
                            if debug:
                                sc.dma("y", lambda e, hg=hg, j=j: e.dma_start(
                                    out=d_tau[j, :, 4 * hg:4 * hg + 4], in_=st_f[:, 0:4]),
                                    after=[tf7])
                                sc.dma("y", lambda e, hg=hg, j=j: e.dma_start(
                                    out=d_rr[j, :, 4 * hg:4 * hg + 4], in_=st_f[:, 20:24]),
                                    after=[tf7])
                        else:
                            tf7 = None
                            tf3 = None

                        oev = [None] * 4
                        new_simuse = [None] * 4
                        prev_exp = None
                        for i in range(4):
                            prev_m4 = m4_hist[-1] if m4_hist else None
                            prev_avlast = av_hist[-1] if av_hist else None
                            h = 4 * hg + i
                            if srch:
                                tm1 = sc.add("v", lambda e, i=i, w=w: e.tensor_scalar(
                                    out=eqs_b[:, 0:w], in0=simt[:, 2048 * i:2048 * i + w],
                                    scalar1=st_f[:, 0 + i:1 + i], scalar2=float(2 ** -12),
                                    op0=ALU.is_equal, op1=ALU.mult),
                                    after=[tf7, prev_eqs, prev_m4])
                                tm2 = sc.add("v", lambda e, i=i, w=w: e.tensor_tensor_scan(
                                    out=cum_f[:, 0:w], data0=eqs_b[:, 0:w],
                                    data1=zeros_b[:, 0:w], initial=st_f[:, 28 + i:29 + i],
                                    op0=ALU.add, op1=ALU.add),
                                    after=[tm1, prev_cum])
                                tm3 = sc.add("v", lambda e, i=i, w=w: e.tensor_tensor(
                                    out=m01_b[:, 0:w], in0=simt[:, 2048 * i:2048 * i + w],
                                    in1=cum_f[:, 0:w], op=ALU.is_ge),
                                    after=[tm2, prev_m4])
                                m_items += [tm1, tm2]
                                prev_cum = tm3
                                prev_eqs = tm2
                                mask_ap = m01_b
                            else:
                                tm3 = None
                                mask_ap = m01j0

                            def scr(e, i=i, qsl=qsl, g=g, ch=ch):
                                for c, (c0, cw) in enumerate(ch):
                                    inst = e.matmul(
                                        ps4[c % 4][:, 0:cw],
                                        qrg[:, qsl + 128 * i:qsl + 128 * (i + 1)],
                                        krT[:, 2048 * g + c0:2048 * g + c0 + cw],
                                        start=True, stop=True)
                                return inst
                            tscr = sc.add("t", scr, after=[tql, prev_ps4])

                            def expf(e, h=h, j=j, ch=ch):
                                for c, (c0, cw) in enumerate(ch):
                                    inst = e.activation(
                                        out=praw_b[:, c0:c0 + cw], in_=ps4[c % 4][:, 0:cw],
                                        func=AF.Exp, scale=INV_SQRT_D,
                                        bias=biasT[:, 32 * j + h:32 * j + h + 1])
                                return inst
                            texp = sc.add("s", expf,
                                          after=[tscr, tf3, prev_exp, prev_m4])
                            prev_exp = texp
                            prev_ps4 = texp

                            if debug and hg == 0 and j == 3 and i == 0:
                                sc.dma("y", lambda e: e.dma_start(out=d_praw[:, 0:1024], in_=praw_b[:, 0:1024]),
                                       after=[texp])
                                sc.dma("y", lambda e: e.dma_start(out=d_m01[:, 0:1024], in_=m01_b[:, 0:1024]),
                                       after=[tm3]) if tm3 else None
                            tm4 = sc.add("v", lambda e, i=i, w=w, mask_ap=mask_ap: e.scalar_tensor_tensor(
                                out=simt[:, 2048 * i:2048 * i + w], in0=praw_b[:, 0:w],
                                scalar=1.0, in1=mask_ap[:, 0:w],
                                op0=ALU.mult, op1=ALU.mult,
                                accum_out=zcol[:, i:i + 1]),
                                after=[texp, tm3, prev_oev[i], sev[i]])
                            if debug and hg == 0 and j == 3 and i == 0:
                                sc.dma("y", lambda e: e.dma_start(out=d_pp[:, 0:1024], in_=simt[:, 0:1024]),
                                       after=[tm4])
                            tr1 = sc.add("v", lambda e, i=i: e.reciprocal(
                                out=zcol[:, 4 + i:5 + i], in_=zcol[:, i:i + 1]),
                                after=[tm4])
                            tm5 = sc.add("v", lambda e, i=i, w=w: e.tensor_scalar(
                                out=simt[:, 2048 * i:2048 * i + w],
                                in0=simt[:, 2048 * i:2048 * i + w],
                                scalar1=zcol[:, 4 + i:5 + i], scalar2=None,
                                op0=ALU.mult), after=[tr1])
                            if debug and hg == 0 and j == 3 and i == 0:
                                sc.dma("y", lambda e: e.dma_start(out=d_z, in_=zcol[:]), after=[tr1])

                            ntile = w // 128
                            avdeps = []
                            av_first = True
                            tav = None
                            for wv0 in range(0, ntile, 8):
                                nw = min(8, ntile - wv0)

                                def trp(e, i=i, wv0=wv0, nw=nw):
                                    for t_ in range(nw):
                                        inst = e.transpose(
                                            ps_t[:, 128 * t_:128 * (t_ + 1)],
                                            simt[:, 2048 * i + 128 * (wv0 + t_):
                                                 2048 * i + 128 * (wv0 + t_ + 1)],
                                            ident_b[:])
                                    return inst
                                ttrp = sc.add("t", trp, after=[tm5, prev_pst])

                                def pev(e, wv0=wv0, nw=nw):
                                    return e.tensor_copy(
                                        out=ptr_b[:, 128 * wv0:128 * (wv0 + nw)],
                                        in_=ps_t[:, 0:128 * nw])
                                tpev = sc.add("v", pev, after=[ttrp, prev_avlast])
                                prev_pst = tpev

                                def av(e, g=g, wv0=wv0, nw=nw, av_first=av_first,
                                       last=(wv0 + 8 >= ntile)):
                                    vv = vtok[:].rearrange("p (tb c) -> p tb c", tb=16)
                                    for t_ in range(nw):
                                        tb = wv0 + t_
                                        inst = e.matmul(
                                            ps_s[:], vv[:, tb, 128 * g:128 * (g + 1)],
                                            ptr_b[:, 128 * tb:128 * (tb + 1)],
                                            start=(av_first and t_ == 0),
                                            stop=(last and t_ == nw - 1))
                                    return inst
                                tav = sc.add("t", av, after=[tpev, prev_pss])
                                av_first = False

                            oev[i] = sc.add("v", lambda e, i=i: e.tensor_copy(
                                out=o_hg[:, 128 * i:128 * (i + 1)], in_=ps_s[:]),
                                after=[tav])
                            prev_pss = oev[i]
                            m4_hist.append(tm5)
                            av_hist.append(tav)
                            new_simuse[i] = tm5

                        prev_simuse = new_simuse
                        prev_m_items = m_items
                        prev_ql[par] = sc.dma("y", lambda e, hg=hg, j=j: e.dma_start(
                            out=o_sp[4 * hg:4 * hg + 4, :, 128 * j:128 * (j + 1)]
                            .rearrange("n p c -> p n c"),
                            in_=o_hg[:].rearrange("p (n c) -> p n c", n=4)),
                            after=oev)
                        prev_osp = prev_ql[par]
                        prev_oev = [prev_osp] * 4

                ph3_dma_done = prev_osp
                if debug:
                    for h in range(H):
                        sc.dma("y", lambda e, h=h: e.dma_start(
                            out=d_o[h], in_=o_sp[h]), after=[prev_osp])
                sc.emit("ph3")

        mid.close()

        def ln_block(sc, tld, ln_x, ln_y, ln_c, deps_sq=None):
            ts1 = sc.add("v", lambda e: e.reduce_sum(
                ln_c[:, 0:1], ln_x[:], axis=AX.X), after=[tld])
            ts2 = sc.add("s", lambda e: e.activation(
                out=ln_y[:], in_=ln_x[:], func=AF.Square,
                accum_out=ln_c[:, 1:2]), after=[tld] + (deps_sq or []))
            ts3 = sc.add("v", lambda e: e.tensor_scalar(
                out=ln_c[:, 2:3], in0=ln_c[:, 0:1], scalar1=1.0 / 4096, scalar2=None,
                op0=ALU.mult), after=[ts1])
            ts4 = sc.add("v", lambda e: e.tensor_scalar(
                out=ln_c[:, 3:4], in0=ln_c[:, 1:2], scalar1=1.0 / 4096, scalar2=None,
                op0=ALU.mult), after=[ts2])
            ts5 = sc.add("v", lambda e: e.tensor_mul(
                out=ln_c[:, 4:5], in0=ln_c[:, 2:3], in1=ln_c[:, 2:3]), after=[ts3])
            ts6 = sc.add("v", lambda e: e.tensor_sub(
                out=ln_c[:, 5:6], in0=ln_c[:, 3:4], in1=ln_c[:, 4:5]), after=[ts4, ts5])
            ts7 = sc.add("v", lambda e: e.tensor_scalar(
                out=ln_c[:, 6:7], in0=ln_c[:, 5:6], scalar1=EPS, scalar2=None,
                op0=ALU.add), after=[ts6])
            ts8 = sc.add("v", lambda e: e.reciprocal(
                out=ln_c[:, 7:8], in_=ln_c[:, 6:7]), after=[ts7])
            ts9 = sc.add("s", lambda e: e.activation(
                out=ln_c[:, 8:9], in_=ln_c[:, 7:8], func=AF.Sqrt), after=[ts8])
            ta1 = sc.add("v", lambda e: e.tensor_scalar(
                out=ln_y[:], in0=ln_x[:], scalar1=ln_c[:, 2:3], scalar2=ln_c[:, 8:9],
                op0=ALU.subtract, op1=ALU.mult), after=[ts9, ts2])
            return ta1

        def stream_matmul_phase(sc, Wdram, lhs_tiles, resid_sp, out_sp, wload,
                                extra_mm=None, extra_deps=None, resid_ready=None):
            wv = wload[:].rearrange("p (n c) -> p n c", n=32)
            wstg = stream_bufs["wstg"]
            xrc = stream_bufs["xrc"]
            yc = stream_bufs["yc"]
            t_w = [None] * 8
            t_wc = [None] * 8
            t_mm = {}
            t_ev = {}
            t_dma = {}
            last_ev_par = [None, None]
            last_dma_par = [None, None]
            for c in range(8):
                t_w[c] = sc.dma("y", lambda e, c=c: e.dma_start(
                    out=wstg[:], in_=Wdram[c]),
                    after=[t_wc[c - 1] if c else None] + stream_bufs.get("wstg_war", []))
                t_wc[c] = sc.add("v", lambda e: e.tensor_copy(out=wload[:], in_=wstg[:]),
                                 after=[t_w[c], t_mm.get((c - 1, NB - 1))])
                for j in range(NB):
                    par = j % 2
                    t_xr = sc.dma("y", lambda e, c=c, j=j, par=par: e.dma_start(
                        out=xrc[:, 512 * par:512 * (par + 1)],
                        in_=resid_sp[j][:, 512 * c:512 * (c + 1)]),
                        after=[last_ev_par[par], resid_ready])

                    def mmf(e, c=c, j=j):
                        ps = ps_a if j % 2 == 0 else ps_b
                        last = extra_mm is None
                        for ei in range(32):
                            inst = e.matmul(ps[:], lhs_tiles(ei, j),
                                            wv[:, ei, :],
                                            start=(ei == 0), stop=(last and ei == 31))
                        if extra_mm is not None:
                            inst = extra_mm(e, ps, c, j)
                        return inst
                    t_mm[(c, j)] = sc.add("t", mmf,
                                          after=[t_wc[c], last_ev_par[par]] + (extra_deps(j) if extra_deps else []))

                    def ev(e, c=c, j=j, par=par):
                        ps = ps_a if j % 2 == 0 else ps_b
                        return e.tensor_add(out=yc[:, 512 * par:512 * (par + 1)],
                                            in0=ps[:], in1=xrc[:, 512 * par:512 * (par + 1)])
                    t_ev[(c, j)] = sc.add("v", ev,
                                          after=[t_mm[(c, j)], t_xr, last_dma_par[par]])
                    last_ev_par[par] = t_ev[(c, j)]
                    t_dma[(c, j)] = sc.dma("y", lambda e, c=c, j=j, par=par: e.dma_start(
                        out=out_sp[j][:, 512 * c:512 * (c + 1)],
                        in_=yc[:, 512 * par:512 * (par + 1)]), after=[t_ev[(c, j)]])
                    last_dma_par[par] = t_dma[(c, j)]
            return t_dma

        if n_phases >= 4:
            with ExitStack() as p4ctx:
                sbl = mk(p4ctx)
                o_all = sbl("o_all", [128, NB * 4096], BF16)
                wload = sbl("wload4", [128, 16384], BF16)
                stream_bufs = {"wstg": sbl("wstg4", [128, 16384], F32),
                               "xrc": sbl("xrc", [128, 2 * 512], F32),
                               "yc": sbl("y1c", [128, 2 * 512], F32)}

                sc = Sched(nc, store, "p4a")
                ov = o_all[:].rearrange("p (j n c) -> p j n c", j=NB, n=32)
                t_ol = [None] * NB
                for j in range(NB):
                    t_ol[j] = sc.dma("y", lambda e, j=j: e.dma_start(
                        out=ov[:, j, :, :],
                        in_=o_sp[:, :, 128 * j:128 * (j + 1)].rearrange("n p c -> p n c")),
                        after=[ph3_dma_done])
                wo_dma = stream_matmul_phase(
                    sc, wo_t, lambda ei, j: ov[:, j, ei, :], x_own, x2_sp, wload,
                    extra_deps=lambda j: [t_ol[j]])
                sc.emit("ph4a")

        mid2 = ExitStack()
        octx.enter_context(mid2)
        x2Tb = mk(mid2)("x2Tb", [128, 32 * 1024], BF16)

        if n_phases >= 4:
            with ExitStack() as p4bctx:
                sbl = mk(p4bctx)
                ln_x = sbl("ln_x4", [128, 4096], F32)
                ln_y = sbl("ln_y4", [128, 4096], F32)
                ln_c = sbl("ln_c4", [128, 16], F32)
                x2b = sbl("x2b4", [128, 4096], BF16)

                sc = Sched(nc, store, "p4b")
                prev_tr = None
                prev_st = None
                prev_apply = None
                for j in range(NB):
                    tld = sc.dma("y", lambda e, j=j: e.dma_start(
                        out=ln_x[:], in_=x2_sp[j]),
                        after=[wo_dma[(c, j)] for c in range(8)] + [prev_st, prev_apply])
                    ta1 = ln_block(sc, tld, ln_x, ln_y, ln_c, deps_sq=[prev_st])
                    prev_apply = ta1
                    ta2 = sc.add("v", lambda e: e.tensor_copy(out=x2b[:], in_=ln_y[:]),
                                 after=[ta1, prev_tr])
                    tst = sc.dma("y", lambda e, j=j: e.dma_start(
                        out=x2_sp[j], in_=ln_y[:]), after=[ta1])
                    prev_st = tst
                    if debug:
                        sc.dma("y", lambda e, j=j: e.dma_start(out=d_x2[j], in_=ln_y[:]),
                               after=[ta1])
                    for wv0 in range(0, 32, 8):
                        ttr = sc.add("t", lambda e, wv0=wv0: [
                            e.transpose(ps_t[:, 128 * t_:128 * (t_ + 1)],
                                        x2b[:, 128 * (wv0 + t_):128 * (wv0 + t_ + 1)],
                                        ident_b[:]) for t_ in range(8)][-1],
                            after=[ta2, prev_tr])
                        tpe = sc.add("v", lambda e, wv0=wv0, j=j: e.tensor_copy(
                            out=x2Tb[:].rearrange("p (n c) -> p n c", n=32)
                            [:, wv0:wv0 + 8, 128 * j:128 * (j + 1)],
                            in_=ps_t[:].rearrange("p (n c) -> p n c", n=8)),
                            after=[ttr])
                        prev_tr = tpe
                ph4b_dma_done = prev_st
                sc.emit("ph4b")

        if n_phases >= 5:
            with ExitStack() as p5ctx:
                sbl = mk(p5ctx)
                wload = sbl("wload5", [128, 16384], BF16)
                gwT_b = sbl("gwT_b", [128, 32 * 4], BF16)
                lAT_b = sbl("lAT_b", [128, 32 * 32], BF16)
                lBp_b = sbl("lBp_b", [32, 4096], BF16)
                ebias_s = sbl("ebias_s", [128, 4], F32)
                lg_f = sbl("lg_f", [128, 8], F32)
                oh_f = sbl("oh_f", [128, 8], F32)
                zms = sbl("zms", [128, NB * 32], BF16)
                zmT = sbl("zmT", [32, NB * 128], BF16)
                stream_bufs = {"wstg": sbl("wstg5", [128, 16384], F32),
                               "xrc": sbl("x2rc", [128, 2 * 512], F32),
                               "yc": sbl("y2c", [128, 2 * 512], F32)}
                psdbg = sbl("psdbg", [128, 64], F32)

                sc = Sched(nc, store, "p5")
                wstg5 = stream_bufs["wstg"]
                tl1 = sc.dma("y", lambda e: e.dma_start(out=wstg5[:, 0:128], in_=gwT_in))
                tc1 = sc.add("v", lambda e: e.tensor_copy(out=gwT_b[:], in_=wstg5[:, 0:128]),
                             after=[tl1])
                tl2 = sc.dma("y", lambda e: e.dma_start(out=wstg5[:, 128:1152], in_=loraAT_in))
                tc2 = sc.add("v", lambda e: e.tensor_copy(out=lAT_b[:], in_=wstg5[:, 128:1152]),
                             after=[tl2])
                tl3 = sc.dma("y", lambda e: e.dma_start(out=wstg5[0:32, 1152:5248], in_=loraBp_in))
                tc3 = sc.add("v", lambda e: e.tensor_copy(out=lBp_b[:], in_=wstg5[0:32, 1152:5248]),
                             after=[tl3])
                tc4 = sc.dma("y", lambda e: e.dma_start(out=ebias_s[:], in_=ebias_in))

                if debug:
                    sc.dma("y", lambda e: e.dma_start(out=d_lat[0], in_=lAT_b[:]), after=[tc2])
                x2v = x2Tb[:].rearrange("p (n c) -> p n c", n=32)
                prev_z = None
                prev_tdoh = None
                prev_tdoh2 = None
                t_zmT = [None] * NB
                for j in range(NB):
                    def gmm(e, j=j):
                        for ei in range(32):
                            inst = e.matmul(ps_s[:, 0:4], x2v[:, ei, 128 * j:128 * (j + 1)],
                                            gwT_b[:, 4 * ei:4 * (ei + 1)],
                                            start=(ei == 0), stop=(ei == 31))
                        return inst
                    tg = sc.add("t", gmm, after=[tc1, prev_z])
                    tb1 = sc.add("v", lambda e: e.tensor_add(
                        out=lg_f[:, 0:4], in0=ps_s[:, 0:4], in1=ebias_s[:]),
                        after=[tg, tc4, prev_z])
                    tdlg = sc.dma("y", lambda e, j=j: e.dma_start(
                        out=d_lg[j], in_=lg_f[:]), after=[tb1]) if debug else None
                    tb2 = sc.add("v", lambda e: e.reduce_max(
                        out=lg_f[:, 4:5], in_=lg_f[:, 0:4], axis=AX.X), after=[tb1])
                    tb3 = sc.add("v", lambda e: e.tensor_scalar(
                        out=oh_f[:, 0:4], in0=lg_f[:, 0:4], scalar1=lg_f[:, 4:5],
                        scalar2=None, op0=ALU.is_ge), after=[tb2, prev_tdoh2])
                    tb4 = sc.add("v", lambda e: e.tensor_copy(
                        out=oh_f[:, 4:5], in_=oh_f[:, 0:1]), after=[tb3])
                    tb5s = [tb4]
                    for ee in range(1, 4):
                        tb5a = sc.add("v", lambda e: e.tensor_scalar(
                            out=oh_f[:, 7:8], in0=oh_f[:, 4:5], scalar1=-1.0, scalar2=1.0,
                            op0=ALU.mult, op1=ALU.add), after=[tb5s[-1]])
                        tb5b = sc.add("v", lambda e, ee=ee: e.tensor_mul(
                            out=oh_f[:, ee:ee + 1], in0=oh_f[:, ee:ee + 1],
                            in1=oh_f[:, 7:8]), after=[tb5a])
                        tb5c = sc.add("v", lambda e, ee=ee: e.tensor_add(
                            out=oh_f[:, 4:5], in0=oh_f[:, 4:5],
                            in1=oh_f[:, ee:ee + 1]), after=[tb5b])
                        tb5s.append(tb5c)

                    tdoh = sc.dma("y", lambda e, j=j: e.dma_start(
                        out=d_oh[j], in_=oh_f[:]), after=[tb5s[-1]]) if debug else None

                    def zmm(e, j=j):
                        for ei in range(32):
                            inst = e.matmul(ps_s[:, 0:32], x2v[:, ei, 128 * j:128 * (j + 1)],
                                            lAT_b[:, 32 * ei:32 * (ei + 1)],
                                            start=(ei == 0), stop=(ei == 31))
                        return inst
                    tz = sc.add("t", zmm, after=[tc2, tb1])
                    prev_tdoh2 = tdoh
                    if debug and j == 0:
                        tq1 = sc.add("v", lambda e: e.tensor_copy(
                            out=psdbg[:, 0:32], in_=ps_s[:, 0:32]), after=[tz])
                        tdelay = sc.add("t", lambda e: e.nop(), after=[tq1])
                        tq2 = sc.add("v", lambda e: e.tensor_copy(
                            out=psdbg[:, 32:64], in_=ps_s[:, 0:32]), after=[tdelay])
                        sc.dma("y", lambda e: e.dma_start(out=d_ps, in_=psdbg[:]), after=[tq2])
                    zm_items = []
                    for ee in range(4):
                        zm_items.append(sc.add("v", lambda e, j=j, ee=ee: e.tensor_scalar(
                            out=zms[:, 32 * j + 8 * ee:32 * j + 8 * (ee + 1)],
                            in0=ps_s[:, 8 * ee:8 * (ee + 1)],
                            scalar1=oh_f[:, ee:ee + 1], scalar2=None,
                            op0=ALU.mult), after=[tz, tb5s[-1]]))
                    prev_z = zm_items[-1]
                    ttr = sc.add("t", lambda e, j=j: e.transpose(
                        ps_t[0:32, 0:128], zms[:, 32 * j:32 * (j + 1)], ident_b[:]),
                        after=zm_items + [t_zmT[j - 1] if j else None])
                    t_zmT[j] = sc.add("v", lambda e, j=j: e.tensor_copy(
                        out=zmT[:, 128 * j:128 * (j + 1)], in_=ps_t[0:32, 0:128]),
                        after=[ttr])

                def ff_extra(e, ps, c, j):
                    return e.matmul(ps[:], zmT[:, 128 * j:128 * (j + 1)],
                                    lBp_b[:, 512 * c:512 * (c + 1)],
                                    start=False, stop=True)
                if debug:
                    sc.dma("y", lambda e: e.dma_start(out=d_zm, in_=zmT[:]), after=[t_zmT[NB - 1]])
                    sc.dma("y", lambda e: e.dma_start(out=d_lat[1], in_=lAT_b[:]), after=[t_zmT[NB - 1]])
                    sc.dma("y", lambda e: e.dma_start(out=d_zms, in_=zms[:]), after=[t_zmT[NB - 1]])
                stream_bufs["wstg_war"] = [tc1, tc2, tc3]
                ph5_last = stream_matmul_phase(
                    sc, wup_t, lambda ei, j: x2v[:, ei, 128 * j:128 * (j + 1)],
                    x2_sp, x3_sp, wload,
                    extra_mm=ff_extra, extra_deps=lambda j: [t_zmT[j], tc3],
                    resid_ready=ph4b_dma_done)
                if debug:
                    for j in range(NB):
                        sc.dma("y", lambda e, j=j: e.dma_start(
                            out=d_x3[j], in_=x3_sp[j]),
                            after=[ph5_last[(c, j)] for c in range(8)])
                sc.emit("ph5")

        if n_phases >= 6:
            with ExitStack() as p6ctx:
                sbl = mk(p6ctx)
                ln_x = sbl("ln_x6", [128, 4096], F32)
                ln_y = sbl("ln_y6", [128, 4096], F32)
                ln_c = sbl("ln_c6", [128, 16], F32)

                sc = Sched(nc, store, "p6")
                out_toks = []
                prev_st = None
                prev_apply = None
                for j in range(NB):
                    tld = sc.dma("y", lambda e, j=j: e.dma_start(
                        out=ln_x[:], in_=x3_sp[j]),
                        after=[ph5_last[(c, j)] for c in range(8)] + [prev_st, prev_apply])
                    ta1 = ln_block(sc, tld, ln_x, ln_y, ln_c, deps_sq=[prev_st])
                    prev_apply = ta1
                    tst = sc.dma("y", lambda e, j=j: e.dma_start(
                        out=out[j], in_=ln_y[:]), after=[ta1])
                    prev_st = tst
                    out_toks.append(tst)
                sc.add("y", lambda e: None, after=out_toks)
                sc.emit("ph6")

        if n_phases < 6:
            sc = Sched(nc, store, "pF")
            tok = sc.dma("y", lambda e: e.dma_start(out=out[0][:, 0:256], in_=biasT[:, 0:256]))
            sc.add("y", lambda e: None, after=[tok])
            sc.emit("phF")

    return nc, dbg


def rope_tables():
    inv = 1.0 / (10000.0 ** (np.arange(0, D, 2, dtype=np.float64) / D))
    t = np.arange(S, dtype=np.float64)
    fr = np.outer(t, inv)
    emb = np.concatenate([fr, fr], axis=1)
    return np.cos(emb).astype(np.float32), np.sin(emb).astype(np.float32)


def host_inputs(core, x, Wq, Wk, Wv, Wo, p1, b1, p2, b2, Wup, gw, eb, lA, lB):
    b, p = core // 2, core % 2
    blocks = [2 * j + p for j in range(NB)]
    rows = np.concatenate([np.arange(128 * g, 128 * g + 128) for g in blocks])
    xb = np.ascontiguousarray(x[b])
    xT = np.ascontiguousarray(xb.T)
    cos, sin = rope_tables()
    cosT = np.ascontiguousarray(cos.T)
    sgn = np.where(np.arange(128) < 64, -1.0, 1.0).astype(np.float32)
    sinT2 = np.ascontiguousarray(sin.T * sgn[:, None])

    minv = np.zeros((NB, 128, 256), dtype=np.uint8)
    for j in range(NB):
        w = wj(j)
        tcols = np.arange(w - 256, w)
        sglob = 256 * j + 128 * p + np.arange(128)
        minv[j] = (tcols[None, :] > sglob[:, None]).astype(np.uint8)
    m01j0 = (np.arange(256)[None, :] <= (128 * p + np.arange(128))[:, None]).astype(np.float32)

    exp832 = np.zeros((8, 32), dtype=np.float32)
    for h in range(H):
        exp832[h // 4, h] = 1.0

    WqT = Wq.T
    WkT = Wk.T
    WvT = Wv.T
    WoT = Wo.T
    WupT = Wup.T
    xT_own = xT[:, rows]
    loraAT = lA.transpose(2, 0, 1).reshape(4096, 32)
    gwT = gw.T
    return {
        "xt_own": np.ascontiguousarray(
            xT_own.reshape(32, 128, 1024).transpose(1, 0, 2).reshape(128, 32768)),
        "xt_kv": np.ascontiguousarray(
            xT.reshape(32, 128, 2, 1024).transpose(2, 1, 0, 3).reshape(2, 128, 32768)),
        "x_own": np.ascontiguousarray(xb[rows].reshape(NB, 128, 4096)),
        "wq_t": np.ascontiguousarray(
            WqT.reshape(32, 128, 32, 128).transpose(2, 1, 0, 3).reshape(32, 128, 4096)),
        "wk_t": np.ascontiguousarray(
            WkT.reshape(32, 128, 8, 128).transpose(2, 1, 0, 3).reshape(8, 128, 4096)),
        "wv_t": np.ascontiguousarray(
            WvT.reshape(32, 128, 8, 128).transpose(2, 1, 0, 3).reshape(8, 128, 4096)),
        "wo_t": np.ascontiguousarray(
            WoT.reshape(32, 128, 8, 512).transpose(2, 1, 0, 3).reshape(8, 128, 16384)),
        "wup_t": np.ascontiguousarray(
            WupT.reshape(32, 128, 8, 512).transpose(2, 1, 0, 3).reshape(8, 128, 16384)),
        "p1_t": np.ascontiguousarray(p1.transpose(1, 0, 2).reshape(128, 4096)),
        "p2_t": np.ascontiguousarray(p2.transpose(1, 0, 2).reshape(128, 4096)),
        "b1": np.ascontiguousarray(b1[:, 0, :].T),
        "b2": np.ascontiguousarray(b2[:, 0, :].T),
        "cosT_own": np.ascontiguousarray(cosT[:, rows]),
        "sinT2_own": np.ascontiguousarray(sinT2[:, rows]),
        "cosT_kv": cosT,
        "sinT2_kv": sinT2,
        "ident_in": np.eye(128, dtype=np.float32),
        "minv": minv,
        "m01j0": m01j0,
        "exp832": exp832,
        "gw_t": np.ascontiguousarray(
            gwT.reshape(32, 128, 4).transpose(1, 0, 2).reshape(128, 128)),
        "ebias": np.tile(np.asarray(eb, dtype=np.float32)[None, :], (128, 1)),
        "lat_t": np.ascontiguousarray(
            loraAT.reshape(32, 128, 32).transpose(1, 0, 2).reshape(128, 1024)),
        "loraBp": np.ascontiguousarray(lB.transpose(0, 2, 1).reshape(32, 4096)),
    }, rows


def kernel(L0_attn_hidden_states, L0_attn_Wq, L0_attn_Wk, L0_attn_Wv, L0_attn_Wo,
           L0_attn_hash_proj1, L0_attn_hash_bias1, L0_attn_hash_proj2, L0_attn_hash_bias2,
           L0_ffn_W_up, L0_ffn_gate_w, L0_ffn_expert_bias, L0_ffn_lora_A, L0_ffn_lora_B):
    args = (L0_attn_hidden_states, L0_attn_Wq, L0_attn_Wk, L0_attn_Wv, L0_attn_Wo,
            L0_attn_hash_proj1, L0_attn_hash_bias1, L0_attn_hash_proj2, L0_attn_hash_bias2,
            L0_ffn_W_up, L0_ffn_gate_w, L0_ffn_expert_bias, L0_ffn_lora_A, L0_ffn_lora_B)
    nc, _ = build_program()
    in_maps, rowss = [], []
    for c in range(8):
        d, rows = host_inputs(c, *[np.asarray(a) for a in args])
        in_maps.append(d)
        rowss.append(rows)
    res = run_bass_kernel_spmd(nc, in_maps, core_ids=list(range(8)))
    full = np.empty((B, S, E), dtype=np.float32)
    for c in range(8):
        full[c // 2, rowss[c]] = np.asarray(res.results[c]["out"]).reshape(1024, 4096)
    return full
